# version 05 (final)
# baseline (speedup 1.0000x reference)
import numpy as np
import ml_dtypes
import concourse.bass as bass
import concourse.mybir as mybir
from concourse.bass_utils import run_bass_kernel_spmd
from concourse.bass_core_rust import IndirectOffsetOnAxis

F32 = mybir.dt.float32
BF16 = mybir.dt.bfloat16
I32 = mybir.dt.int32
AF = mybir.ActivationFunctionType
AL = mybir.AluOpType
BF = ml_dtypes.bfloat16

B, T, D = 8, 4096, 1024
N_CORES = 8
SEM_EPOCH_LIMIT = 30000

ENGINES = ("tensor", "vector", "scalar", "gpsimd", "sync")


class _Op:
    __slots__ = ("idx", "engine", "emit", "dma", "cc", "need_inc",
                 "sem", "waits", "tag", "lane")

    def __init__(self, idx, engine, emit, dma, cc, tag):
        self.idx = idx
        self.engine = engine
        self.emit = emit
        self.dma = dma
        self.cc = cc
        self.need_inc = False
        self.sem = None
        self.waits = {}
        self.tag = tag
        self.lane = None


class _Lane:
    __slots__ = ("nc", "name", "n", "handle", "count", "waiters")

    def __init__(self, nc, name):
        self.nc = nc
        self.name = name
        self.n = 0
        self.handle = None
        self.count = 0
        self.waiters = []

    def next_dma(self):
        rot = None
        if self.handle is None or self.count + 16 > SEM_EPOCH_LIMIT:
            if self.handle is not None:
                rot = (self.handle, self.count)
            self.handle = self.nc.alloc_semaphore(f"{self.name}_{self.n}")
            self.n += 1
            self.count = 0
        self.count += 16
        return (self.handle, self.count, rot)


import bisect


class _Compute:
    __slots__ = ("nc", "name", "n", "handle", "count", "midx", "mtok")

    def __init__(self, nc, name):
        self.nc = nc
        self.name = name
        self.n = 0
        self.handle = None
        self.count = 0
        self.midx = []
        self.mtok = []

    def token_for(self, idx):
        if self.midx and self.midx[-1] >= idx:
            pos = bisect.bisect_left(self.midx, idx)
            return self.midx[pos], self.mtok[pos]
        if self.handle is None or self.count + 1 > SEM_EPOCH_LIMIT:
            self.handle = self.nc.alloc_semaphore(f"{self.name}_{self.n}")
            self.n += 1
            self.count = 0
        self.count += 1
        tok = (self.handle, self.count)
        self.midx.append(idx)
        self.mtok.append(tok)
        return idx, tok


class EB:

    N_LANES = 12

    def __init__(self, name="mk"):
        self.nc = bass.Bass("TRN2", target_bir_lowering=False, name=name)
        self.ops = []
        self.last_writer = {}
        self.readers = {}
        self.alloc_tag = {}
        self.pending_free = set()
        self._uid = 0
        self._scopes = []
        self._const_cache = {}
        self._ident = None
        self._ident_b = None
        self.out_aps = []
        self.comp = {e: _Compute(self.nc, f"c_{e}") for e in ENGINES}
        self.lanes = {}
        self.known = {e: {} for e in ENGINES}

    def uid(self, prefix):
        self._uid += 1
        return f"{prefix}_{self._uid}"

    def sbuf(self, prefix, shape, dtype, persistent=False):
        g = self.nc.sbuf_tensor(self.uid(prefix), shape, dtype,
                                side="right" if persistent else "left")
        h = g.__enter__()
        if self._scopes and not persistent:
            self._scopes[-1].append((g, h))
        if self.pending_free:
            self.alloc_tag[h.name] = frozenset(self.pending_free)
        return h

    def psum(self, prefix, shape, dtype=F32):
        g = self.nc.psum_tensor(self.uid(prefix), shape, dtype)
        h = g.__enter__()
        if self._scopes:
            self._scopes[-1].append((g, h))
        if self.pending_free:
            self.alloc_tag[h.name] = frozenset(self.pending_free)
        return h

    def dram(self, prefix, shape, dtype, kind="Internal"):
        return self.nc.dram_tensor(self.uid(prefix), shape, dtype, kind=kind)

    class _Scope:
        def __init__(self, eb):
            self.eb = eb

        def __enter__(self):
            self.eb._scopes.append([])
            return self

        def __exit__(self, *a):
            items = self.eb._scopes.pop()
            for g, h in reversed(items):
                lw = self.eb.last_writer.get(h.name)
                if lw is not None:
                    self.eb.pending_free.add(lw)
                for r in self.eb.readers.get(h.name, ()):
                    self.eb.pending_free.add(r)
                g.__exit__(None, None, None)
            return False

    def scope(self):
        return EB._Scope(self)

    def const_dram(self, arr, name):
        key = name
        if key not in self._const_cache:
            self._const_cache[key] = self.nc.inline_tensor(
                np.ascontiguousarray(arr), self.uid(name))
        return self._const_cache[key]

    def const_sb(self, arr, name, dtype=None):
        key = "sb_" + name
        if key not in self._const_cache:
            arr = np.ascontiguousarray(arr)
            if arr.ndim == 1:
                arr = arr[None, :]
            dt_ = dtype or {np.dtype(np.float32): F32,
                            np.dtype(np.int32): I32,
                            np.dtype(BF): BF16}[arr.dtype]
            dr = self.nc.inline_tensor(arr, self.uid(name + "_c"))
            h = self.sbuf(name, list(arr.shape), dt_, persistent=True)
            eng = "gpsimd" if (arr.dtype == np.dtype(BF)) else "sync"
            self.dma(h.ap(), dr.ap(), engine="sync")
            self._const_cache[key] = h
        return self._const_cache[key]

    def identity_bf16(self):
        if self._ident is None:
            eye = np.eye(128, dtype=BF)
            self._ident = self.const_sb(eye.astype(np.float32), "ident_f")
            h = self.sbuf("ident_b", [128, 128], BF16, persistent=True)
            self.vop("tensor_copy", h[:], self._ident[:])
            self._ident_b = h
        return self._ident_b

    @staticmethod
    def _key(x):
        t = getattr(x, "tensor", None)
        if t is not None:
            return t.name
        n = getattr(x, "name", None)
        return n

    def _dep_wait(self, d, engine, waits, lane_registrants):
        p = self.ops[d]
        if p.cc:
            h, v = p.sem
            if waits.get(h, 0) < v:
                waits[h] = v
            return
        if p.dma:
            lane = p.lane
            if waits.get(lane.handle, 0) < lane.count:
                waits[lane.handle] = lane.count
            lane_registrants.append(lane)
            return
        if p.engine == "tensor" and engine == "tensor":
            return
        midx, (h, v) = self.comp[p.engine].token_for(d)
        mop = self.ops[midx]
        mop.need_inc = True
        mop.sem = (h, v)
        if waits.get(h, 0) < v:
            waits[h] = v

    def op(self, engine, emit, reads=(), writes=(), dma=False, cc=False, tag=""):
        deps = set()
        rk = [k for k in (self._key(r) for r in reads) if k is not None]
        wk = [k for k in (self._key(w) for w in writes) if k is not None]
        for k in rk:
            lw = self.last_writer.get(k)
            if lw is not None:
                deps.add(lw)
        for k in wk:
            lw = self.last_writer.get(k)
            if lw is not None:
                deps.add(lw)
            deps.update(self.readers.get(k, ()))
            tg = self.alloc_tag.pop(k, None)
            if tg:
                deps.update(tg)
        idx = len(self.ops)
        deps.discard(idx)
        o = _Op(idx, engine, emit, dma, cc, tag)
        self.ops.append(o)

        red = {}
        dma_deps = []
        for d in deps:
            p = self.ops[d]
            if p.dma or p.cc:
                dma_deps.append(d)
            else:
                key = p.engine
                if key not in red or red[key] < d:
                    red[key] = d
        waits = {}
        lane_regs = []
        for d in sorted(red.values()) + sorted(dma_deps):
            self._dep_wait(d, engine, waits, lane_regs)
        kn = self.known[engine]
        o.waits = {}
        for h, v in waits.items():
            if kn.get(h, 0) < v:
                o.waits[h] = v
                kn[h] = v

        if cc:
            h = self.nc.alloc_semaphore(self.uid("cc"))
            o.sem = (h, 16)
            o.need_inc = True
        elif dma:
            rr, lanes = self.lanes.setdefault(
                engine, [0, [_Lane(self.nc, f"d_{engine}{i}")
                             for i in range(self.N_LANES)]])[0], \
                self.lanes[engine][1]
            lane = lanes[rr % self.N_LANES]
            self.lanes[engine][0] = rr + 1
            gate_waits = {}
            gate_regs = []
            for w in lane.waiters:
                if w < idx:
                    self._dep_wait(w, engine, gate_waits, gate_regs)
            lane.waiters = []
            for lr in gate_regs:
                lr.waiters.append(idx)
            for h, v in gate_waits.items():
                if kn.get(h, 0) < v and o.waits.get(h, 0) < v:
                    o.waits[h] = v
                    kn[h] = v
            h_, v_, rot = lane.next_dma()
            if rot is not None and o.waits.get(rot[0], 0) < rot[1]:
                o.waits[rot[0]] = rot[1]
                kn[rot[0]] = max(kn.get(rot[0], 0), rot[1])
            o.sem = (h_, v_)
            o.lane = lane
            o.need_inc = True
        for lr in lane_regs:
            lr.waiters.append(idx)

        for k in rk:
            self.readers.setdefault(k, []).append(idx)
        for k in wk:
            self.last_writer[k] = idx
            self.readers[k] = []
        return o

    def dma(self, out, in_, engine="sync", tag="", extra_reads=(), **kw):
        def emit(e):
            return e.dma_start(out=out, in_=in_, **kw)
        return self.op(engine, emit, reads=(in_,) + tuple(extra_reads),
                       writes=(out,), dma=True, tag=tag)

    def idma(self, out, in_, out_off=None, in_off=None, tag="", **kw):
        reads = [in_]
        if in_off is not None:
            reads.append(in_off.ap)
        writes = [out]
        w_extra = []
        if out_off is not None:
            reads.append(out_off.ap)

        def emit(e):
            return e.indirect_dma_start(out, out_off, in_, in_off, **kw)
        return self.op("gpsimd", emit, reads=tuple(reads), writes=tuple(writes),
                       dma=True, tag=tag)

    def mm(self, out, lhsT, rhs, start=True, stop=True, tag=""):
        def emit(e):
            return e.matmul(out, lhsT, rhs, start=start, stop=stop)
        return self.op("tensor", emit, reads=(lhsT, rhs), writes=(out,), tag=tag)

    def trans(self, out_ps, in_sb, tag=""):
        ident = self.identity_bf16()
        k = in_sb.shape[0]

        def emit(e):
            return e.transpose(out_ps, in_sb, ident[0:k, 0:k])
        return self.op("tensor", emit, reads=(in_sb, ident), writes=(out_ps,),
                       tag=tag)

    def vop(self, meth, out, *ins, engine="vector", extra_reads=(),
            extra_writes=(), tag="", **kw):
        def emit(e):
            return getattr(e, meth)(out, *ins, **kw)
        return self.op(engine, emit, reads=tuple(ins) + tuple(extra_reads),
                       writes=(out,) + tuple(extra_writes), tag=tag)

    def float_col(self, value):
        key = ("fc", float(value))
        if key not in self._const_cache:
            h = self.sbuf("fc", [128, 1], F32, persistent=True)
            self.memset(h[:], float(value))
            self._const_cache[key] = h
        return self._const_cache[key]

    def act(self, out, in_, func, bias=0.0, scale=1.0, accum_out=None, tag=""):
        if isinstance(bias, float) and bias != 0.0:
            bias = self.float_col(bias)[:]
        reads = [in_]
        if isinstance(bias, bass.AP):
            reads.append(bias)
        if isinstance(scale, bass.AP):
            reads.append(scale)
        writes = [out]
        if accum_out is not None:
            writes.append(accum_out)

        def emit(e):
            return e.activation(out, in_, func, bias=bias, scale=scale,
                                accum_out=accum_out)
        return self.op("scalar", emit, reads=tuple(reads), writes=tuple(writes),
                       tag=tag)

    def memset(self, ap, val, engine="vector", tag=""):
        def emit(e):
            return e.memset(ap, val)
        return self.op(engine, emit, reads=(), writes=(ap,), tag=tag)

    def collective(self, kind, alu, ins, outs, tag="cc"):
        def emit(e):
            return e.collective_compute(kind, alu,
                                        replica_groups=[list(range(N_CORES))],
                                        ins=[a.opt() for a in ins],
                                        outs=[a.opt() for a in outs])
        return self.op("gpsimd", emit, reads=tuple(ins), writes=tuple(outs),
                       dma=True, cc=True, tag=tag)

    def mark_output(self, out_ap):
        self.out_aps.append(out_ap)

    def finalize(self):
        waits = {}
        for eng, (rr, lanes) in self.lanes.items():
            for ln in lanes:
                if ln.handle is not None and waits.get(ln.handle, 0) < ln.count:
                    waits[ln.handle] = ln.count
        for ename in ENGINES:
            eidxs = [o.idx for o in self.ops if o.engine == ename and
                     not o.dma and not o.cc]
            if eidxs:
                midx, (h, v) = self.comp[ename].token_for(eidxs[-1])
                mop = self.ops[midx]
                mop.need_inc = True
                mop.sem = (h, v)
                if waits.get(h, 0) < v:
                    waits[h] = v
        for o in self.ops:
            if o.cc and waits.get(o.sem[0], 0) < 16:
                waits[o.sem[0]] = 16
        term = self.op("sync", lambda e: e.nop(), tag="term")
        term.waits = waits

        by_eng = {e: [] for e in ENGINES}
        for o in self.ops:
            by_eng[o.engine].append(o)

        with self.nc.Block() as block:
            for ename in ENGINES:
                eops = by_eng[ename]

                def body(eng, eops=eops):
                    for o in eops:
                        for h, v in sorted(o.waits.items(),
                                           key=lambda x: str(x[0])):
                            eng.wait_ge(h, v)
                        try:
                            ins = o.emit(eng)
                        except Exception:
                            import sys as _s
                            print(f"EMIT FAIL op idx={o.idx} tag={o.tag} "
                                  f"engine={o.engine}", file=_s.stderr)
                            raise
                        if o.need_inc:
                            ins.then_inc(o.sem[0], 16 if o.dma else 1)
                getattr(block, ename)(body)
        return self.nc



def bcast_row(E, row_ap, ncols, prefix="bc", dtype=F32):
    ones = E.const_sb(np.ones((1, 128), np.float32), "ones_row1")
    onesb = E.const_sb(np.ones((1, 128), np.float32).astype(BF).astype(np.float32),
                       "ones_row1b")
    out = E.sbuf(prefix, [128, ncols], dtype)
    nchunk = (ncols + 511) // 512
    with E.scope():
        ps = E.psum("bc_ps", [128, min(512, ncols)], F32)
        for c in range(nchunk):
            lo, hi = c * 512, min(ncols, (c + 1) * 512)
            E.mm(ps[:, 0:hi - lo], ones[0:1, 0:128], row_ap[0:1, lo:hi])
            E.act(out[:, lo:hi], ps[:, 0:hi - lo], AF.Identity)
    return out


def emit_ln(E, out_ap, x_ap, dcount, eps, rstd_scratch=None,
            scale_mul_bc=None, bias_add_bc=None, prefix="ln"):
    m = E.sbuf(prefix + "_m", [128, 1], F32)
    var = E.sbuf(prefix + "_v", [128, 1], F32)
    sq = E.sbuf(prefix + "_sq", [128, dcount], F32)
    xc = E.sbuf(prefix + "_xc", [128, dcount], F32)
    E.vop("tensor_reduce", m[:], x_ap, axis=mybir.AxisListType.X, op=AL.add)
    E.vop("tensor_scalar_mul", m[:], m[:], 1.0 / dcount)
    E.vop("tensor_scalar", xc[:], x_ap, m[:], None, op0=AL.subtract)
    E.act(sq[:], xc[:], AF.Square, accum_out=var[:])
    E.act(var[:], var[:], AF.Ln, scale=1.0 / dcount, bias=eps)
    E.act(var[:], var[:], AF.Exp, scale=-0.5)
    if scale_mul_bc is not None:
        E.vop("scalar_tensor_tensor", out_ap, xc[:], var[:], scale_mul_bc,
              op0=AL.mult, op1=AL.mult)
        if bias_add_bc is not None:
            E.vop("tensor_add", out_ap, out_ap, bias_add_bc)
    else:
        E.vop("tensor_scalar", out_ap, xc[:], var[:], None, op0=AL.mult)


def gelu_tanh(E, out_ap, in_ap, prefix="gt"):
    E.act(out_ap, in_ap, AF.Gelu_apprx_tanh)


def fm_cols(E, vec_dram, n, prefix="fmc"):
    nch = n // 128
    assert nch <= 128
    out = E.sbuf(prefix, [128, nch], F32)
    with E.scope():
        rows = E.sbuf(prefix + "_r", [nch, 128], F32)
        E.dma(rows[:], vec_dram.ap().rearrange("(c j) -> c j", j=128))
        rows_b = E.sbuf(prefix + "_rb", [nch, 128], BF16)
        E.vop("tensor_copy", rows_b[:], rows[:])
        ps = E.psum(prefix + "_ps", [128, nch], BF16)
        E.trans(ps[:], rows_b[0:nch, :])
        E.act(out[:], ps[:], AF.Identity)
    return out


def load_w_fm(E, w_dram, rows, cols, prefix="w", dtype=BF16, row0=0, col0=0,
              row_stride=None, into=None):
    nch = rows // 128
    out = into if into is not None else E.sbuf(prefix, [128, nch, cols], dtype)
    full_cols = row_stride if row_stride is not None else w_dram.shape[-1]
    base = row0 * full_cols + col0
    ap = bass.AP(tensor=w_dram, offset=base,
                 ap=[[full_cols, 128], [128 * full_cols, nch], [1, cols]])
    E.dma(out[:], ap, engine="gpsimd")
    return out


def transpose_to_fm(E, fm_dst, src_tiles, prefix="t2f"):
    for xt, toff in src_tiles:
        with E.scope():
            if xt.tensor.dtype != BF16:
                xbt = E.sbuf(prefix + "_b", [128, 1024], BF16)
                E.vop("tensor_copy", xbt[:], xt)
                xb = xbt[:]
            else:
                xb = xt
            for h in range(2):
                ps = E.psum(prefix + "_ps", [128, 512], BF16)
                for c in range(4):
                    cc = h * 4 + c
                    E.trans(ps[:, c * 128:(c + 1) * 128],
                            xb[:, cc * 128:(cc + 1) * 128])
                E.act(fm_dst[:, h * 4:h * 4 + 4, toff:toff + 128],
                      ps[:].rearrange("p (c j) -> p c j", j=128), AF.Identity)



def emit_l3_ffn(E, xin, xout, W):
    U1 = load_w_fm(E, W["L3_ffn_U1"], 1024, 256, "u1")
    V1 = load_w_fm(E, W["L3_ffn_V1"], 256, 4096, "v1")
    U2 = load_w_fm(E, W["L3_ffn_U2"], 4096, 256, "u2")
    V2 = load_w_fm(E, W["L3_ffn_V2"], 256, 1024, "v2")
    b1f = fm_cols(E, W["L3_ffn_b1"], 4096, "b1f")
    b2row = E.sbuf("b2r", [1, 1024], F32)
    E.dma(b2row[:], W["L3_ffn_b2"].ap()[None, :])
    b2bc = bcast_row(E, b2row[:], 1024, "b2bc")

    for page in range(8):
        with E.scope():
            xres = [E.sbuf("xres", [128, 1024], F32) for _ in range(4)]
            xfm = E.sbuf("xfm", [128, 8, 512], BF16)
            tiles = []
            for tt in range(4):
                g = page * 4 + tt
                E.dma(xres[tt][:], xin.ap()[g * 128:(g + 1) * 128, :])
                tiles.append((xres[tt][:], tt * 128))
            transpose_to_fm(E, xfm, tiles, "t3f")
            Psb = E.sbuf("P", [128, 2, 512], BF16)
            Qsb = E.sbuf("Q", [128, 2, 512], BF16)
            with E.scope():
                ps = E.psum("Pps", [128, 512], F32)
                for rc in range(2):
                    for c in range(8):
                        E.mm(ps[:], U1[:, c, rc * 128:(rc + 1) * 128],
                             xfm[:, c, :], start=(c == 0), stop=(c == 7))
                    E.act(Psb[:, rc, :], ps[:], AF.Identity)
            with E.scope():
                psQ = [E.psum("Qps", [128, 512], F32) for _ in range(2)]
                psH = E.psum("Hps", [128, 512], F32)
                Hsb = [E.sbuf("H", [128, 512], BF16) for _ in range(2)]
                for hc in range(32):
                    for rc in range(2):
                        E.mm(psH[:], V1[:, rc, hc * 128:(hc + 1) * 128],
                             Psb[:, rc, :], start=(rc == 0), stop=(rc == 1))
                    hb = Hsb[hc % 2]
                    E.act(hb[:], psH[:], AF.Gelu_apprx_tanh,
                          bias=b1f[:, hc:hc + 1])
                    for rc in range(2):
                        E.mm(psQ[rc][:], U2[:, hc, rc * 128:(rc + 1) * 128],
                             hb[:], start=(hc == 0), stop=(hc == 31))
                for rc in range(2):
                    E.act(Qsb[:, rc, :], psQ[rc][:], AF.Identity)
            for tt in range(4):
                with E.scope():
                    ps = [E.psum("Ops", [128, 512], F32) for _ in range(2)]
                    for nh in range(2):
                        for rc in range(2):
                            E.mm(ps[nh][:], Qsb[:, rc, tt * 128:(tt + 1) * 128],
                                 V2[:, rc, nh * 512:(nh + 1) * 512],
                                 start=(rc == 0), stop=(rc == 1))
                    xnew = E.sbuf("xnew", [128, 1024], F32)
                    for nh in range(2):
                        sl = slice(nh * 512, (nh + 1) * 512)
                        E.vop("tensor_add", xnew[:, sl], ps[nh][:], b2bc[:, sl])
                    E.vop("tensor_add", xnew[:], xnew[:], xres[tt][:])
                    og = E.sbuf("og", [128, 1024], F32)
                    emit_ln(E, og[:], xnew[:], 1024, 1e-5, prefix="oln")
                    g = page * 4 + tt
                    E.mark_output(
                        E.dma(xout.ap()[g * 128:(g + 1) * 128, :], og[:]))


def load_w_fm_T(E, w_dram, nrows, ncols, prefix="wt"):
    ncc, nrc = ncols // 128, nrows // 128
    out = E.sbuf(prefix, [128, ncc, nrows], BF16)
    for rc in range(nrc):
        with E.scope():
            rows = E.sbuf(prefix + "_r", [128, ncols], BF16)
            E.dma(rows[:], w_dram.ap()[rc * 128:(rc + 1) * 128, :],
                  engine="gpsimd")
            for half in range(ncc // 4 + (1 if ncc % 4 else 0)):
                ps = E.psum(prefix + "_ps", [128, 512], BF16)
                cs = range(half * 4, min(ncc, half * 4 + 4))
                for i, cc in enumerate(cs):
                    E.trans(ps[:, i * 128:(i + 1) * 128],
                            rows[:, cc * 128:(cc + 1) * 128])
                for i, cc in enumerate(cs):
                    E.vop("tensor_copy", out[:, cc, rc * 128:(rc + 1) * 128],
                          ps[:, i * 128:(i + 1) * 128])
    return out



def emit_l1_ffn(E, xin, xout, W):
    WrT = load_w_fm_T(E, W["L1_ffn_W_r"], 1024, 1024, "wrT")
    WkT = load_w_fm_T(E, W["L1_ffn_W_k"], 1024, 1024, "wkT")
    WvT = load_w_fm_T(E, W["L1_ffn_W_v"], 1024, 1024, "wvT")
    mur = fm_cols(E, W["L1_ffn_mu_r"], 1024, "mur")
    muk = fm_cols(E, W["L1_ffn_mu_k"], 1024, "muk")

    x_fm = E.sbuf("xfm1", [128, 8, T], BF16)
    for g in range(32):
        with E.scope():
            xt = E.sbuf("xt", [128, 1024], F32)
            E.dma(xt[:], xin.ap()[g * 128:(g + 1) * 128, :])
            transpose_to_fm(E, x_fm, [(xt[:], g * 128)], "l1t")

    PG = 512
    for page in range(T // PG):
        with E.scope():
            lo = page * PG
            xs = E.sbuf("xs", [128, 8, PG], BF16)
            if page == 0:
                E.memset(xs[:, :, 0:1], 0.0)
                E.vop("tensor_copy", xs[:, :, 1:PG], x_fm[:, :, 0:PG - 1])
            else:
                E.vop("tensor_copy", xs[:], x_fm[:, :, lo - 1:lo + PG - 1])
            r_tm = E.sbuf("r", [128, PG // 128, 1024], BF16)
            kact = E.sbuf("ka", [128, 8, PG], BF16)
            with E.scope():
                xr = E.sbuf("xr", [128, 8, PG], BF16)
                xk = E.sbuf("xk", [128, 8, PG], BF16)
                for c in range(8):
                    t0 = E.sbuf("t0", [128, PG], BF16)
                    E.vop("tensor_sub", t0[:], x_fm[:, c, lo:lo + PG],
                          xs[:, c, :])
                    E.vop("scalar_tensor_tensor", xr[:, c, :], t0[:],
                          mur[:, c:c + 1], xs[:, c, :],
                          op0=AL.mult, op1=AL.add)
                    E.vop("scalar_tensor_tensor", xk[:, c, :], t0[:],
                          muk[:, c:c + 1], xs[:, c, :],
                          op0=AL.mult, op1=AL.add)
                psp = [E.psum("rps", [128, 512], F32) for _ in range(2)]
                i = 0
                for tt in range(PG // 128):
                    for nh in range(2):
                        ps = psp[i % 2]
                        i += 1
                        for c in range(8):
                            E.mm(ps[:], xr[:, c, tt * 128:(tt + 1) * 128],
                                 WrT[:, c, nh * 512:(nh + 1) * 512],
                                 start=(c == 0), stop=(c == 7))
                        E.act(r_tm[:, tt, nh * 512:(nh + 1) * 512], ps[:],
                              AF.Sigmoid)
                t1 = [E.sbuf("t1", [128, PG], BF16) for _ in range(2)]
                for dc in range(8):
                    ps = psp[i % 2]
                    i += 1
                    for c in range(8):
                        E.mm(ps[:, 0:PG], WkT[:, c, dc * 128:(dc + 1) * 128],
                             xk[:, c, :], start=(c == 0), stop=(c == 7))
                    tb = t1[i % 2]
                    E.vop("tensor_scalar_max", tb[:], ps[:, 0:PG], 0.0)
                    E.act(kact[:, dc, :], tb[:], AF.Square)
            for tt in range(PG // 128):
                g = lo // 128 + tt
                with E.scope():
                    ps = [E.psum("ops", [128, 512], F32) for _ in range(2)]
                    for nh in range(2):
                        for c in range(8):
                            E.mm(ps[nh][:], kact[:, c, tt * 128:(tt + 1) * 128],
                                 WvT[:, c, nh * 512:(nh + 1) * 512],
                                 start=(c == 0), stop=(c == 7))
                    xres = E.sbuf("xres", [128, 1024], F32)
                    E.dma(xres[:], xin.ap()[g * 128:(g + 1) * 128, :])
                    xnew = E.sbuf("xnew", [128, 1024], F32)
                    for nh in range(2):
                        sl = slice(nh * 512, (nh + 1) * 512)
                        E.vop("tensor_mul", xnew[:, sl], ps[nh][:],
                              r_tm[:, tt, sl])
                    E.vop("tensor_add", xnew[:], xnew[:], xres[:])
                    og = E.sbuf("og", [128, 1024], F32)
                    emit_ln(E, og[:], xnew[:], 1024, 1e-5, prefix="oln")
                    E.dma(xout.ap()[g * 128:(g + 1) * 128, :], og[:])
    E.mark_output(xout.ap())



def emit_l0_attn(E, xin, xout, W):
    x1buf = E.dram("x1buf", [T, D], F32)
    with E.scope():
        _l0_attn_phase1(E, xin, x1buf, W)
    with E.scope():
        _l0_attn_phase2(E, xin, x1buf, xout, W)
    E.mark_output(xout.ap())


def _l0_attn_phase1(E, xin, x1buf, W):
    wq = load_w_fm(E, W["L0_attn_wq"], 1024, 1024, "wq", row_stride=1024)
    wk = load_w_fm(E, W["L0_attn_wk"], 1024, 1024, "wk", row_stride=1024)
    wv = load_w_fm(E, W["L0_attn_wv"], 1024, 1024, "wv", row_stride=1024)
    wo = load_w_fm(E, W["L0_attn_wo"], 1024, 1024, "wo", row_stride=1024)
    ln1s_r = E.sbuf("ln1s", [1, 1024], F32)
    E.dma(ln1s_r[:], W["L0_attn_ln1_scale"].ap()[None, :])
    ln1b_r = E.sbuf("ln1b", [1, 1024], F32)
    E.dma(ln1b_r[:], W["L0_attn_ln1_bias"].ap()[None, :])
    ln1s = bcast_row(E, ln1s_r[:], 1024, "ln1sb")
    ln1b = bcast_row(E, ln1b_r[:], 1024, "ln1bb")
    ltri = np.tril(np.ones((128, 128), np.float32))
    ltri_c = E.const_sb(np.triu(np.ones((128, 128), np.float32)), "triu")
    ones_col = E.const_sb(np.ones((128, 1), np.float32), "ones_col")
    ones_col_b = E.sbuf("onescb", [128, 1], BF16, persistent=True)
    E.vop("tensor_copy", ones_col_b[:], ones_col[:])

    S_sb = E.sbuf("S0", [128, 8, 66], F32, persistent=True)
    E.memset(S_sb[:], 0.0)

    def hslice(t_, h, cols):
        return t_[(h % 2) * 64:(h % 2) * 64 + 64, h // 2, cols]

    for chunk in range(8):
        c0 = chunk * 512
        with E.scope():
            xn_fm = E.sbuf("xnfm", [128, 8, 512], BF16)
            for tt in range(4):
                with E.scope():
                    g = chunk * 4 + tt
                    xt = E.sbuf("xt", [128, 1024], F32)
                    E.dma(xt[:], xin.ap()[g * 128:(g + 1) * 128, :])
                    xn = E.sbuf("xn", [128, 1024], F32)
                    emit_ln(E, xn[:], xt[:], 1024, 1e-6,
                            scale_mul_bc=ln1s[:], bias_add_bc=ln1b[:],
                            prefix="ln1")
                    transpose_to_fm(E, xn_fm, [(xn[:], tt * 128)], "l0t")
            phq = E.sbuf("phq", [128, 8, 512], BF16)
            phk = E.sbuf("phk", [128, 8, 512], BF16)
            with E.scope():
                psp = [E.psum("qkps", [128, 512], F32) for _ in range(2)]
                for oc in range(8):
                    for wsel, dst in ((wq, phq), (wk, phk)):
                        ps = psp[oc % 2]
                        for c in range(8):
                            E.mm(ps[:], wsel[:, c, oc * 128:(oc + 1) * 128],
                                 xn_fm[:, c, :], start=(c == 0), stop=(c == 7))
                        E.vop("tensor_scalar", dst[:, oc, :], ps[:],
                              0.0, 0.001, op0=AL.max, op1=AL.add)
            v_tm = E.sbuf("vtm", [128, 4, 16, 66], BF16)
            E.memset(v_tm[:, :, :, 64:66], 1.0)
            pk_tm = E.sbuf("pktm", [128, 4, 1024], BF16)
            with E.scope():
                psp = [E.psum("vps", [128, 512], F32) for _ in range(2)]
                pst = [E.psum("tps", [128, 512], BF16) for _ in range(2)]
                for i in range(4):
                    for nh in range(2):
                        ps = psp[nh]
                        for c in range(8):
                            E.mm(ps[:], xn_fm[:, c, i * 128:(i + 1) * 128],
                                 wv[:, c, nh * 512:(nh + 1) * 512],
                                 start=(c == 0), stop=(c == 7))
                        E.vop("tensor_copy",
                              v_tm[:, i, nh * 8:(nh + 1) * 8, 0:64],
                              ps[:].rearrange("p (h d) -> p h d", d=64))
                    for nh in range(2):
                        ps = pst[nh]
                        for c in range(4):
                            cc = nh * 4 + c
                            E.trans(ps[:, c * 128:(c + 1) * 128],
                                    phk[:, cc, i * 128:(i + 1) * 128])
                        E.vop("tensor_copy",
                              pk_tm[:, i, nh * 512:(nh + 1) * 512], ps[:])
            attn_tm = E.sbuf("attn", [128, 4, 16, 64], F32)
            for i in range(4):
                with E.scope():
                    At = E.sbuf("At", [128, 16, 128], BF16)
                    with E.scope():
                        psA = [E.psum("psA", [128, 128], F32) for _ in range(2)]
                        for h in range(16):
                            ps = psA[h % 2]
                            E.mm(ps[:], hslice(phk, h, slice(i * 128, (i + 1) * 128)),
                                 hslice(phq, h, slice(i * 128, (i + 1) * 128)))
                            E.vop("tensor_mul", At[:, h, :], ps[:], ltri_c[:])
                    nsum = E.sbuf("nsum", [128, 16, 66], F32)
                    S_b = E.sbuf("S_b", [128, 8, 66], BF16)
                    E.vop("tensor_copy", S_b[:], S_sb[:])
                    with E.scope():
                        nA = [E.psum("nA", [128, 7, 66], F32) for _ in range(3)]
                        nB = [E.psum("nB", [128, 7, 66], F32) for _ in range(3)]
                        for h in range(16):
                            bA = nA[h // 7][:, h % 7, :]
                            bB = nB[h // 7][:, h % 7, :]
                            E.mm(bA, At[:, h, :], v_tm[:, i, h, :])
                            E.mm(bB, hslice(phq, h, slice(i * 128, (i + 1) * 128)),
                                 S_b[(h % 2) * 64:(h % 2) * 64 + 64, h // 2, :])
                        for b in range(3):
                            hn = 7 if b < 2 else 2
                            tmp = E.sbuf("nBs", [128, 7, 66], F32)
                            E.act(tmp[:, 0:hn, :], nB[b][:, 0:hn, :], AF.Identity)
                            E.vop("tensor_add", nsum[:, b * 7:b * 7 + hn, :],
                                  nA[b][:, 0:hn, :], tmp[:, 0:hn, :])
                    rec = E.sbuf("rec", [128, 16, 1], F32)
                    E.vop("reciprocal", rec[:], nsum[:, :, 64:65])
                    E.vop("tensor_mul", attn_tm[:, i, :, :], nsum[:, :, 0:64],
                          rec[:].broadcast_to([128, 16, 64]))
                    with E.scope():
                        psS = [E.psum("psS", [64, 7, 66], F32) for _ in range(3)]
                        for h in range(16):
                            E.mm(psS[h // 7][0:64, h % 7, :],
                                 pk_tm[:, i, h * 64:(h + 1) * 64],
                                 v_tm[:, i, h, :])
                        for h in range(16):
                            E.vop("tensor_add",
                                  S_sb[(h % 2) * 64:(h % 2) * 64 + 64, h // 2, :],
                                  S_sb[(h % 2) * 64:(h % 2) * 64 + 64, h // 2, :],
                                  psS[h // 7][0:64, h % 7, :])
            for i in range(4):
                with E.scope():
                    af = E.sbuf("af", [128, 8, 128], BF16)
                    transpose_to_fm(E, af, [(attn_tm[:, i, :, :].rearrange(
                        "p h d -> p (h d)"), 0)], "l0a")
                    ps = [E.psum("prj", [128, 512], F32) for _ in range(2)]
                    for nh in range(2):
                        for c in range(8):
                            E.mm(ps[nh][:], af[:, c, :],
                                 wo[:, c, nh * 512:(nh + 1) * 512],
                                 start=(c == 0), stop=(c == 7))
                    g = chunk * 4 + i
                    x0t = E.sbuf("x0t", [128, 1024], F32)
                    E.dma(x0t[:], xin.ap()[g * 128:(g + 1) * 128, :])
                    x1t = E.sbuf("x1t", [128, 1024], F32)
                    for nh in range(2):
                        sl = slice(nh * 512, (nh + 1) * 512)
                        E.vop("tensor_add", x1t[:, sl], ps[nh][:], x0t[:, sl])
                    E.dma(x1buf.ap()[g * 128:(g + 1) * 128, :], x1t[:])


def _l0_attn_phase2(E, xin, x1buf, xout, W):
    w1db = [E.sbuf("w1db", [128, 8, 128], BF16) for _ in range(2)]
    w2 = load_w_fm(E, W["L0_attn_w2"], 4096, 1024, "w2")
    b1f = fm_cols(E, W["L0_attn_b1"], 4096, "b1f0")
    b2r = E.sbuf("b2r0", [1, 1024], F32)
    E.dma(b2r[:], W["L0_attn_b2"].ap()[None, :])
    b2bc = bcast_row(E, b2r[:], 1024, "b2bc0")
    ln2s_r = E.sbuf("ln2s", [1, 1024], F32)
    E.dma(ln2s_r[:], W["L0_attn_ln2_scale"].ap()[None, :])
    ln2b_r = E.sbuf("ln2b", [1, 1024], F32)
    E.dma(ln2b_r[:], W["L0_attn_ln2_bias"].ap()[None, :])
    ln2s = bcast_row(E, ln2s_r[:], 1024, "ln2sb")
    ln2b = bcast_row(E, ln2b_r[:], 1024, "ln2bb")

    for page in range(8):
        with E.scope():
            xn_fm = E.sbuf("x2fm", [128, 8, 512], BF16)
            x1res = [E.sbuf("x1res", [128, 1024], F32) for _ in range(4)]
            for tt in range(4):
                with E.scope():
                    g = page * 4 + tt
                    E.dma(x1res[tt][:], x1buf.ap()[g * 128:(g + 1) * 128, :])
                    xn = E.sbuf("xn2", [128, 1024], F32)
                    emit_ln(E, xn[:], x1res[tt][:], 1024, 1e-6,
                            scale_mul_bc=ln2s[:], bias_add_bc=ln2b[:],
                            prefix="ln2")
                    transpose_to_fm(E, xn_fm, [(xn[:], tt * 128)], "l0m")
            hsb = E.sbuf("hsb", [128, 32, 512], BF16)
            with E.scope():
                psp = [E.psum("hps", [128, 512], F32) for _ in range(2)]
                for hc in range(32):
                    wb = w1db[hc % 2]
                    load_w_fm(E, W["L0_attn_w1"], 1024, 128,
                              col0=hc * 128, row_stride=4096, into=wb)
                    ps = psp[hc % 2]
                    for c in range(8):
                        E.mm(ps[:], wb[:, c, :],
                             xn_fm[:, c, :], start=(c == 0), stop=(c == 7))
                    E.act(hsb[:, hc, :], ps[:], AF.Gelu_apprx_tanh,
                          bias=b1f[:, hc:hc + 1])
            for tt in range(4):
                with E.scope():
                    ps = [E.psum("yps", [128, 512], F32) for _ in range(2)]
                    for nh in range(2):
                        for hc in range(32):
                            E.mm(ps[nh][:], hsb[:, hc, tt * 128:(tt + 1) * 128],
                                 w2[:, hc, nh * 512:(nh + 1) * 512],
                                 start=(hc == 0), stop=(hc == 31))
                    xnew = E.sbuf("xnew0", [128, 1024], F32)
                    for nh in range(2):
                        sl = slice(nh * 512, (nh + 1) * 512)
                        E.vop("tensor_add", xnew[:, sl], ps[nh][:], b2bc[:, sl])
                    E.vop("tensor_add", xnew[:], xnew[:], x1res[tt][:])
                    x0t2 = E.sbuf("x0t2", [128, 1024], F32)
                    E.dma(x0t2[:], xin.ap()[g * 128 + page * 0:(g + 1) * 128, :]
                          if False else xin.ap()[(page * 4 + tt) * 128:
                                                 (page * 4 + tt + 1) * 128, :])
                    E.vop("tensor_add", xnew[:], xnew[:], x0t2[:])
                    og = E.sbuf("og0", [128, 1024], F32)
                    emit_ln(E, og[:], xnew[:], 1024, 1e-5, prefix="oln0")
                    g = page * 4 + tt
                    E.dma(xout.ap()[g * 128:(g + 1) * 128, :], og[:])



def emit_l3_attn(E, xin, xout, W):
    WqkvT = load_w_fm_T(E, W["L3_attn_W_qkv"], 3072, 1024, "wqkvT")
    WoT = load_w_fm_T(E, W["L3_attn_W_out"], 1024, 1024, "woT")
    bm = np.zeros((128, 128), np.float32)
    bm[0:64, 64:128] = 1.0
    bmask = E.const_sb(bm, "bmask")
    S_sb = E.sbuf("S3", [128, 8, 66], F32, persistent=True)
    E.memset(S_sb[:], 0.0)

    def hslice(t_, h, cols):
        return t_[(h % 2) * 64:(h % 2) * 64 + 64, h // 2, cols]

    for chunk in range(8):
        with E.scope():
            x_fm = E.sbuf("x3fm", [128, 8, 512], BF16)
            xres = [E.sbuf("x3res", [128, 1024], F32) for _ in range(4)]
            for tt in range(4):
                g = chunk * 4 + tt
                E.dma(xres[tt][:], xin.ap()[g * 128:(g + 1) * 128, :])
                transpose_to_fm(E, x_fm, [(xres[tt][:], tt * 128)], "l3t")
            phq = E.sbuf("phq3", [128, 8, 512], BF16)
            phk = E.sbuf("phk3", [128, 8, 512], BF16)
            v_tm = E.sbuf("vtm3", [128, 4, 16, 66], BF16)
            E.memset(v_tm[:, :, :, 64:66], 1.0)
            pk_tm = E.sbuf("pktm3", [128, 4, 1024], BF16)
            with E.scope():
                psp = [E.psum("qps3", [128, 512], F32) for _ in range(2)]
                for oc in range(8):
                    ps = psp[oc % 2]
                    for c in range(8):
                        E.mm(ps[:], WqkvT[:, c, 1024 + oc * 128:1024 + (oc + 1) * 128],
                             x_fm[:, c, :], start=(c == 0), stop=(c == 7))
                    E.act(phk[:, oc, :], ps[:], AF.Exp)
                for i in range(4):
                    tsl = slice(i * 128, (i + 1) * 128)
                    qtm = E.sbuf("qtm3", [128, 1024], F32)
                    for nh in range(2):
                        ps = psp[nh]
                        for c in range(8):
                            E.mm(ps[:], x_fm[:, c, tsl],
                                 WqkvT[:, c, nh * 512:(nh + 1) * 512],
                                 start=(c == 0), stop=(c == 7))
                        E.act(qtm[:, nh * 512:(nh + 1) * 512], ps[:],
                              AF.Identity)
                    qv = qtm[:].rearrange("p (h d) -> p h d", d=64)
                    mx = E.sbuf("mx3", [128, 16], F32)
                    E.vop("tensor_reduce", mx[:], qv, axis=mybir.AxisListType.X,
                          op=AL.max)
                    qs = E.sbuf("qs3", [128, 16, 64], F32)
                    E.vop("tensor_sub", qs[:], qv,
                          mx[:, :, None].broadcast_to([128, 16, 64]))
                    sm = E.sbuf("sm3", [128, 16], F32)
                    E.act(qs[:].rearrange("p h d -> p (h d)"),
                          qs[:].rearrange("p h d -> p (h d)"), AF.Exp)
                    E.vop("tensor_reduce", sm[:], qs[:],
                          axis=mybir.AxisListType.X, op=AL.add)
                    E.vop("reciprocal", sm[:], sm[:])
                    E.vop("tensor_scalar_mul", sm[:], sm[:], 0.125)
                    qb = E.sbuf("qb3", [128, 1024], BF16)
                    E.vop("tensor_mul", qb[:].rearrange("p (h d) -> p h d", d=64),
                          qs[:], sm[:, :, None].broadcast_to([128, 16, 64]))
                    transpose_to_fm(E, phq, [(qb[:], i * 128)], "l3q")
                    for nh in range(2):
                        ps = psp[nh]
                        for c in range(8):
                            E.mm(ps[:], x_fm[:, c, tsl],
                                 WqkvT[:, c, 2048 + nh * 512:2048 + (nh + 1) * 512],
                                 start=(c == 0), stop=(c == 7))
                        E.vop("tensor_copy",
                              v_tm[:, i, nh * 8:(nh + 1) * 8, 0:64],
                              ps[:].rearrange("p (h d) -> p h d", d=64))
                pst = [E.psum("tps3", [128, 512], BF16) for _ in range(2)]
                for i in range(4):
                    for nh in range(2):
                        ps = pst[nh]
                        for c in range(4):
                            cc = nh * 4 + c
                            E.trans(ps[:, c * 128:(c + 1) * 128],
                                    phk[:, cc, i * 128:(i + 1) * 128])
                        E.vop("tensor_copy",
                              pk_tm[:, i, nh * 512:(nh + 1) * 512], ps[:])
            attn_tm = E.sbuf("attn3", [128, 4, 16, 64], F32)
            for i in range(4):
                with E.scope():
                    At = E.sbuf("At3", [128, 16, 128], BF16)
                    with E.scope():
                        psA = [E.psum("psA3", [128, 128], F32) for _ in range(2)]
                        for h in range(16):
                            ps = psA[h % 2]
                            E.mm(ps[:], hslice(phk, h, slice(i * 128, (i + 1) * 128)),
                                 hslice(phq, h, slice(i * 128, (i + 1) * 128)))
                            E.vop("tensor_mul", At[:, h, :], ps[:], bmask[:])
                    nsum = E.sbuf("nsum3", [128, 16, 66], F32)
                    S_b = E.sbuf("S_b3", [128, 8, 66], BF16)
                    E.vop("tensor_copy", S_b[:], S_sb[:])
                    with E.scope():
                        nA = [E.psum("nA3", [128, 7, 66], F32) for _ in range(3)]
                        nB = [E.psum("nB3", [128, 7, 66], F32) for _ in range(3)]
                        for h in range(16):
                            E.mm(nA[h // 7][:, h % 7, :], At[:, h, :],
                                 v_tm[:, i, h, :])
                            E.mm(nB[h // 7][:, h % 7, :],
                                 hslice(phq, h, slice(i * 128, (i + 1) * 128)),
                                 S_b[(h % 2) * 64:(h % 2) * 64 + 64, h // 2, :])
                        for b in range(3):
                            hn = 7 if b < 2 else 2
                            tmp = E.sbuf("nBs3", [128, 7, 66], F32)
                            E.act(tmp[:, 0:hn, :], nB[b][:, 0:hn, :], AF.Identity)
                            E.vop("tensor_add", nsum[:, b * 7:b * 7 + hn, :],
                                  nA[b][:, 0:hn, :], tmp[:, 0:hn, :])
                    rec = E.sbuf("rec3", [128, 16, 1], F32)
                    E.vop("tensor_scalar_max", rec[:], nsum[:, :, 64:65], 1e-6)
                    E.vop("reciprocal", rec[:], rec[:])
                    E.vop("tensor_mul", attn_tm[:, i, :, :], nsum[:, :, 0:64],
                          rec[:].broadcast_to([128, 16, 64]))
                    with E.scope():
                        psS = [E.psum("psS3", [64, 7, 66], F32) for _ in range(3)]
                        for h in range(16):
                            E.mm(psS[h // 7][0:64, h % 7, :],
                                 pk_tm[:, i, h * 64:(h + 1) * 64],
                                 v_tm[:, i, h, :])
                        for h in range(16):
                            E.vop("tensor_add",
                                  S_sb[(h % 2) * 64:(h % 2) * 64 + 64, h // 2, :],
                                  S_sb[(h % 2) * 64:(h % 2) * 64 + 64, h // 2, :],
                                  psS[h // 7][0:64, h % 7, :])
            for i in range(4):
                with E.scope():
                    af = E.sbuf("af3", [128, 8, 128], BF16)
                    transpose_to_fm(E, af, [(attn_tm[:, i, :, :].rearrange(
                        "p h d -> p (h d)"), 0)], "l3a")
                    ps = [E.psum("prj3", [128, 512], F32) for _ in range(2)]
                    for nh in range(2):
                        for c in range(8):
                            E.mm(ps[nh][:], af[:, c, :],
                                 WoT[:, c, nh * 512:(nh + 1) * 512],
                                 start=(c == 0), stop=(c == 7))
                    xnew = E.sbuf("xnew3", [128, 1024], F32)
                    for nh in range(2):
                        sl = slice(nh * 512, (nh + 1) * 512)
                        E.vop("tensor_add", xnew[:, sl], ps[nh][:],
                              xres[i][:, sl])
                    og = E.sbuf("og3", [128, 1024], F32)
                    emit_ln(E, og[:], xnew[:], 1024, 1e-5, prefix="oln3")
                    g = chunk * 4 + i
                    E.dma(xout.ap()[g * 128:(g + 1) * 128, :], og[:])
    E.mark_output(xout.ap())



GAMMA = 0.96875


def emit_l2_attn_partial(E, xin, m_part, W):
    ones_f = E.const_sb(np.ones((1, 128), np.float32), "ones_row1")
    x_fm = E.sbuf("x2afm", [128, 8, T], BF16)
    for g in range(32):
        with E.scope():
            xt = E.sbuf("xt2", [128, 1024], F32)
            E.dma(xt[:], xin.ap()[g * 128:(g + 1) * 128, :])
            transpose_to_fm(E, x_fm, [(xt[:], g * 128)], "l2t")
    with E.scope():
        Wk = load_w_fm(E, W["L2_attn_Wk"], 1024, 1024, "wk2")
        wvs = E.sbuf("wvs2", [128, 8], BF16)
        with E.scope():
            wv = load_w_fm(E, W["L2_attn_Wv"], 1024, 1024, "wv2")
            t = E.sbuf("wvt", [128, 8], F32)
            for c in range(8):
                E.vop("tensor_reduce", t[:, c:c + 1], wv[:, c, :],
                      axis=mybir.AxisListType.X, op=AL.add)
            E.vop("tensor_copy", wvs[:], t[:])
        for pg in range(8):
            tsl = slice(pg * 512, (pg + 1) * 512)
            with E.scope():
                psv = E.psum("vsps", [1, 512], F32)
                for c in range(8):
                    E.mm(psv[:], wvs[:, c:c + 1], x_fm[:, c, tsl],
                         start=(c == 0), stop=(c == 7))
                vsr = E.sbuf("vsr", [1, 512], F32)
                E.act(vsr[:], psv[:], AF.Identity)
                vsb_ps = E.psum("vsb", [128, 512], F32)
                E.mm(vsb_ps[:], ones_f[0:1, :], vsr[:])
                vsb = E.sbuf("vsbs", [128, 512], F32)
                E.act(vsb[:], vsb_ps[:], AF.Identity)
                mfm = E.sbuf("mfm", [128, 8, 512], F32)
                psp = [E.psum("kps2", [128, 512], F32) for _ in range(2)]
                for oc in range(8):
                    ps = psp[oc % 2]
                    for c in range(8):
                        E.mm(ps[:], Wk[:, c, oc * 128:(oc + 1) * 128],
                             x_fm[:, c, tsl], start=(c == 0), stop=(c == 7))
                    E.vop("tensor_mul", mfm[:, oc, :], ps[:], vsb[:])
                E.dma(m_part.ap().rearrange("(c p) t -> p c t", p=128)[:, :, tsl],
                      mfm[:])
    E.mark_output(m_part.ap())


def emit_l2_attn_rest(E, xin, mparts, xout, W):
    ones_f = E.const_sb(np.ones((1, 128), np.float32), "ones_row1")
    m_sum = E.dram("m_sum", [1024, T], F32)
    s_tm = E.dram("s_tm", [T, 1024], BF16)
    with E.scope():
        acc = E.sbuf("macc", [128, T], F32)
        t1 = E.sbuf("mt1", [128, T], F32)
        for c in range(8):
            rap = lambda mp: mp.ap().rearrange("(c p) t -> p c t", p=128)[:, c, :]
            E.dma(acc[:], rap(mparts[0]))
            for b in range(1, len(mparts)):
                E.dma(t1[:], rap(mparts[b]))
                E.vop("tensor_add", acc[:], acc[:], t1[:])
            E.dma(m_sum.ap().rearrange("(c p) t -> p c t", p=128)[:, c, :],
                  acc[:])
    gam = E.sbuf("gam", [128, T], F32, persistent=False)
    E.memset(gam[:], GAMMA)
    for c in range(8):
        with E.scope():
            mch = E.sbuf("mch", [128, T], F32)
            E.dma(mch[:], m_sum.ap().rearrange("(c p) t -> p c t", p=128)[:, c, :])
            sch = E.sbuf("sch", [128, T], F32)
            E.vop("tensor_tensor_scan", sch[:, 1:T], gam[:, 1:T],
                  mch[:, 1:T], 0.0, op0=AL.mult, op1=AL.add)
            E.vop("tensor_copy", sch[:, 0:1], sch[:, 1:2])
            sb16 = E.sbuf("sb16", [128, T], BF16)
            E.vop("tensor_copy", sb16[:], sch[:])
            pst = [E.psum("stps", [128, 512], BF16) for _ in range(2)]
            stl = [E.sbuf("stl", [128, 4, 128], BF16) for _ in range(2)]
            for q4 in range(8):
                ps = pst[q4 % 2]
                sl = stl[q4 % 2]
                for q in range(4):
                    g = q4 * 4 + q
                    E.trans(ps[:, q * 128:(q + 1) * 128],
                            sb16[:, g * 128:(g + 1) * 128])
                E.vop("tensor_copy", sl[:], ps[:].rearrange(
                    "p (q j) -> p q j", j=128))
                dap = bass.AP(tensor=s_tm, offset=q4 * 512 * 1024 + c * 128,
                              ap=[[1024, 128], [128 * 1024, 4], [1, 128]])
                E.dma(dap, sl[:])
    Wq = load_w_fm(E, W["L2_attn_Wq"], 1024, 1024, "wq2")
    x_fm = E.sbuf("x2bfm", [128, 8, T], BF16)
    for g in range(32):
        with E.scope():
            xt = E.sbuf("xt2b", [128, 1024], F32)
            E.dma(xt[:], xin.ap()[g * 128:(g + 1) * 128, :])
            transpose_to_fm(E, x_fm, [(xt[:], g * 128)], "l2bt")
    for pg in range(8):
        tsl = slice(pg * 512, (pg + 1) * 512)
        with E.scope():
            rpg = E.sbuf("rpg", [128, 8, 512], BF16)
            base = (pg // 2) * 1024 + (pg % 2) * 512
            ap = bass.AP(tensor=s_tm, offset=base,
                         ap=[[4 * 1024, 128], [512 * 1024, 8], [1, 512]])
            E.dma(rpg[:], ap)
            ofm = E.sbuf("ofm", [128, 8, 512], BF16)
            psp = [E.psum("qps2", [128, 512], F32) for _ in range(2)]
            for oc in range(8):
                ps = psp[oc % 2]
                for c in range(8):
                    E.mm(ps[:], Wq[:, c, oc * 128:(oc + 1) * 128],
                         x_fm[:, c, tsl], start=(c == 0), stop=(c == 7))
                E.vop("tensor_mul", ofm[:, oc, :], ps[:], rpg[:, oc, :])
            pst = [E.psum("otps", [128, 512], BF16) for _ in range(2)]
            for tt in range(4):
                g = pg * 4 + tt
                with E.scope():
                    xres = E.sbuf("x2res", [128, 1024], F32)
                    E.dma(xres[:], xin.ap()[g * 128:(g + 1) * 128, :])
                    xnew = E.sbuf("xnew2", [128, 1024], F32)
                    for nh in range(2):
                        ps = pst[nh]
                        for c in range(4):
                            cc = nh * 4 + c
                            E.trans(ps[:, c * 128:(c + 1) * 128],
                                    ofm[:, cc, tt * 128:(tt + 1) * 128])
                        E.vop("tensor_add", xnew[:, nh * 512:(nh + 1) * 512],
                              ps[:], xres[:, nh * 512:(nh + 1) * 512])
                    og = E.sbuf("og2", [128, 1024], F32)
                    emit_ln(E, og[:], xnew[:], 1024, 1e-5, prefix="oln2")
                    E.dma(xout.ap()[g * 128:(g + 1) * 128, :], og[:])
    E.mark_output(xout.ap())



def hslice_l1(t_, h, cols):
    return t_[(h % 2) * 64:(h % 2) * 64 + 64, h // 2, cols]


def emit_l1_attn(E, xin, xout, W):
    HN, EN = 8, 4
    ones_f = E.const_sb(np.ones((1, 128), np.float32), "ones_row1")
    idf = E.const_sb(np.eye(128, dtype=np.float32), "ident_f32")
    bm = np.full((128, 160), -1e9, np.float32)
    for tl in range(128):
        bm[tl, tl + 1:tl + 33] = 0.0
    bmadd = E.const_sb(bm, "l1_bandmask")
    hsel = np.zeros((32, EN), np.float32)
    for he in range(32):
        hsel[he, he % 4] = 1.0
    hsel_c = E.const_sb(hsel, "l1_hsel")
    esel = np.zeros((4, 4 * 128), np.float32)
    for e in range(EN):
        esel[e, e * 128:(e + 1) * 128] = 1.0
    esel_c = E.const_sb(esel, "l1_esel")

    gkv_dram = E.dram("gkv1", [EN, 128, T], F32)
    vext_dram = E.dram("vext1", [4224, EN, 64], BF16)
    Q_fm = E.sbuf("qfm1", [128, 4, T], BF16)
    nKV_bf = E.sbuf("nkv1", [128, EN, 4128], BF16)
    A_em = E.sbuf("aem1", [EN, T], F32)

    def transpose_f32(out_ps, in_sb):
        def emit_t(eng, o=out_ps, i_=in_sb):
            k = i_.shape[0]
            return eng.transpose(o, i_, idf[0:k, 0:k])
        E.op("tensor", emit_t, reads=(in_sb, idf[:]), writes=(out_ps,))

    with E.scope():
        wKVT = E.sbuf("wkvT1", [128, 8, 1024], BF16)
        for h in range(HN):
            with E.scope():
                rows = E.sbuf("kvr", [128, 1024], BF16)
                E.dma(rows[0:64, :], W["L1_attn_w_K"].ap().rearrange(
                    "h d c -> (h d) c")[h * 64:(h + 1) * 64, :], engine="gpsimd")
                E.dma(rows[64:128, :], W["L1_attn_w_V"].ap().rearrange(
                    "h d c -> (h d) c")[h * 64:(h + 1) * 64, :], engine="gpsimd")
                for half in range(2):
                    ps = E.psum("kvps", [128, 512], BF16)
                    for i in range(4):
                        cc = half * 4 + i
                        E.trans(ps[:, i * 128:(i + 1) * 128],
                                rows[:, cc * 128:(cc + 1) * 128])
                    for i in range(4):
                        cc = half * 4 + i
                        E.vop("tensor_copy", wKVT[:, cc, h * 128:(h + 1) * 128],
                              ps[:, i * 128:(i + 1) * 128])
        wQT = E.sbuf("wqT1", [128, 8, 512], BF16)
        for rc in range(4):
            with E.scope():
                rows = E.sbuf("wqr", [128, 1024], BF16)
                ap = bass.AP(tensor=W["L1_attn_w_Q"], offset=rc * 128 * 1024,
                             ap=[[1024, 128], [1, 1024]])
                E.dma(rows[:], ap, engine="gpsimd")
                for half in range(2):
                    ps = E.psum("wqps", [128, 512], BF16)
                    for i in range(4):
                        cc = half * 4 + i
                        E.trans(ps[:, i * 128:(i + 1) * 128],
                                rows[:, cc * 128:(cc + 1) * 128])
                    for i in range(4):
                        cc = half * 4 + i
                        E.vop("tensor_copy",
                              wQT[:, cc, rc * 128:(rc + 1) * 128],
                              ps[:, i * 128:(i + 1) * 128])
        wGT = E.sbuf("wgT1", [128, 8, 32], BF16)
        with E.scope():
            rows = E.sbuf("wgr", [32, 1024], BF16)
            E.dma(rows[:], W["L1_attn_w_G"].ap().rearrange("h e c -> (h e) c"),
                  engine="gpsimd")
            ps = E.psum("wgps", [128, 8, 32], BF16)
            for cc in range(8):
                E.trans(ps[:, cc, :], rows[:, cc * 128:(cc + 1) * 128])
            E.vop("tensor_copy", wGT[:], ps[:])
        bG = E.sbuf("bG1", [32, 1], F32)
        E.dma(bG[:], W["L1_attn_b_G"].ap().rearrange("h e -> (h e)")[:, None])

        x_fm = E.sbuf("x1afm", [128, 8, T], BF16)
        G_fm = E.sbuf("gfm1", [32, T], F32)
        for g in range(32):
            with E.scope():
                xt = E.sbuf("xt1", [128, 1024], F32)
                E.dma(xt[:], xin.ap()[g * 128:(g + 1) * 128, :])
                transpose_to_fm(E, x_fm, [(xt[:], g * 128)], "l1at")
        for pg in range(8):
            tsl = slice(pg * 512, (pg + 1) * 512)
            with E.scope():
                psp = [E.psum("pps1", [128, 512], F32) for _ in range(2)]
                for oc in range(4):
                    ps = psp[oc % 2]
                    for c in range(8):
                        E.mm(ps[:], wQT[:, c, oc * 128:(oc + 1) * 128],
                             x_fm[:, c, tsl], start=(c == 0), stop=(c == 7))
                    E.act(Q_fm[:, oc, tsl], ps[:], AF.Identity, scale=0.125)
                psg = E.psum("gps1", [32, 512], F32)
                for c in range(8):
                    E.mm(psg[:], wGT[:, c, :], x_fm[:, c, tsl],
                         start=(c == 0), stop=(c == 7))
                E.act(G_fm[0:32, tsl], psg[:], AF.Sigmoid, bias=bG[:])
        with E.scope():
            gb = E.sbuf("gb1", [32, T], BF16)
            E.vop("tensor_copy", gb[:], G_fm[:])
            psa = [E.psum("aps1", [EN, 512], F32) for _ in range(2)]
            hsb = E.sbuf("hselb", [32, EN], BF16)
            E.vop("tensor_copy", hsb[:], hsel_c[:])
            for pg in range(8):
                ps = psa[pg % 2]
                E.mm(ps[:], hsb[:], gb[:, pg * 512:(pg + 1) * 512])
                E.vop("tensor_scalar", A_em[0:EN, pg * 512:(pg + 1) * 512],
                      ps[:], 1.0, -1.0, op0=AL.min, op1=AL.mult)
            E.vop("tensor_scalar_add", A_em[:], A_em[:], 1.0)
        for g in range(32):
            with E.scope():
                kvfm = E.sbuf("kvfm", [128, 8, 128], BF16)
                psp = [E.psum("kfps", [128, 128], F32) for _ in range(2)]
                for h in range(HN):
                    ps = psp[h % 2]
                    for c in range(8):
                        E.mm(ps[:], wKVT[:, c, h * 128:(h + 1) * 128],
                             x_fm[:, c, g * 128:(g + 1) * 128],
                             start=(c == 0), stop=(c == 7))
                    E.vop("tensor_copy", kvfm[:, h, :], ps[:])
                kvtm = E.sbuf("kvtm", [128, 8, 128], BF16)
                pst = [E.psum("ktps", [128, 512], BF16) for _ in range(2)]
                for half in range(2):
                    ps = pst[half]
                    for i in range(4):
                        E.trans(ps[:, i * 128:(i + 1) * 128],
                                kvfm[:, half * 4 + i, :])
                    E.vop("tensor_copy", kvtm[:, half * 4:half * 4 + 4, :],
                          ps[:].rearrange("p (i j) -> p i j", j=128))
                gtm = E.sbuf("gtm", [128, 32], BF16)
                psgt = E.psum("gtps", [128, 32], BF16)
                gslice = E.sbuf("gsl", [32, 128], BF16)
                E.vop("tensor_copy", gslice[:],
                      G_fm[:, g * 128:(g + 1) * 128])
                E.trans(psgt[:], gslice[:])
                E.vop("tensor_copy", gtm[:], psgt[:])
                acc = E.sbuf("gacc", [128, EN, 128], F32)
                t1 = E.sbuf("gt1", [128, EN, 128], F32)
                for h in range(HN):
                    dst = acc if h == 0 else t1
                    E.vop("tensor_mul", dst[:],
                          kvtm[:, h:h + 1, :].broadcast_to([128, EN, 128]),
                          gtm[:, h * 4:(h + 1) * 4, None].broadcast_to(
                              [128, EN, 128]))
                    if h > 0:
                        E.vop("tensor_add", acc[:], acc[:], t1[:])
                pse = E.psum("geps", [128, 128], F32)
                for e in range(EN):
                    transpose_f32(pse[:], acc[:, e, :])
                    ev = E.sbuf("gev", [128, 128], F32)
                    E.act(ev[:], pse[:], AF.Identity)
                    dap = bass.AP(tensor=gkv_dram,
                                  offset=e * 128 * T + g * 128,
                                  ap=[[T, 128], [1, 128]])
                    E.dma(dap, ev[:])

    with E.scope():
        iKV = E.sbuf("ikv1", [128, EN, 32], F32)
        with E.scope():
            ik = E.sbuf("ikr", [32, EN, 64], F32)
            E.dma(ik[:], W["L1_attn_init_K_rec"].ap().rearrange(
                "e l d -> l e d"))
            iv = E.sbuf("ivr", [32, EN, 64], F32)
            E.dma(iv[:], W["L1_attn_init_V_rec"].ap().rearrange(
                "e l d -> l e d"))
            ivb = E.sbuf("ivb", [32, EN, 64], BF16)
            E.vop("tensor_copy", ivb[:], iv[:])
            for e in range(EN):
                psi = E.psum("ips", [64, 32], F32)
                transpose_f32(psi[0:64, :], ik[:, e, :])
                E.act(iKV[0:64, e, :], psi[0:64, :], AF.Identity)
                psi2 = E.psum("ips2", [64, 32], F32)
                transpose_f32(psi2[0:64, :], iv[:, e, :])
                E.act(iKV[64:128, e, :], psi2[0:64, :], AF.Identity)
            E.dma(vext_dram.ap()[0:32, :, :], ivb[:])
            zz = E.sbuf("zz1", [96, EN * 64], BF16)
            E.memset(zz[:], 0.0)
            E.dma(vext_dram.ap()[4128:4224, :, :].rearrange(
                "r e d -> r (e d)"), zz[:])
        for e in range(EN):
            with E.scope():
                gch = E.sbuf("gch1", [128, T], F32)
                E.dma(gch[:], gkv_dram.ap()[e, :, :])
                aex = E.sbuf("aex1", [128, T], F32)
                psb = E.psum("abps", [128, 512], F32)
                for pg in range(8):
                    E.mm(psb[:], esel_c[0:4, e * 128:(e + 1) * 128],
                         A_em[0:4, pg * 512:(pg + 1) * 512])
                    E.act(aex[:, pg * 512:(pg + 1) * 512], psb[:], AF.Identity)
                t2 = E.sbuf("inj1", [128, 32], F32)
                E.vop("tensor_mul", t2[:], aex[:, ::128], iKV[:, e, :])
                E.vop("tensor_add", gch[:, ::128], gch[:, ::128], t2[:])
                E.memset(aex[:, ::128], 0.0)
                nkv = E.sbuf("nkvs", [128, T], F32)
                E.vop("tensor_tensor_scan", nkv[:], aex[:], gch[:], 0.0,
                      op0=AL.mult, op1=AL.add)
                E.vop("tensor_copy", nKV_bf[0:64, e, 32:4128], nkv[0:64, :])
                E.vop("tensor_copy", nKV_bf[64:128, e, 32:4128], nkv[0:64, :])
                E.vop("tensor_copy", nKV_bf[0:64, e, 0:32], iKV[0:64, e, :])
                E.vop("tensor_copy", nKV_bf[64:128, e, 0:32], iKV[0:64, e, :])
                nvb = E.sbuf("nvb1", [64, T], BF16)
                E.vop("tensor_copy", nvb[:], nkv[64:128, :])
                pst = [E.psum("vtps", [128, 64], BF16) for _ in range(2)]
                for g in range(32):
                    ps = pst[g % 2]
                    E.trans(ps[:], nvb[:, g * 128:(g + 1) * 128])
                    ev = E.sbuf("vev1", [128, 64], BF16)
                    E.vop("tensor_copy", ev[:], ps[:])
                    dap = bass.AP(tensor=vext_dram,
                                  offset=(32 + g * 128) * EN * 64 + e * 64,
                                  ap=[[EN * 64, 128], [1, 64]])
                    E.dma(dap, ev[:])

    with E.scope():
        wO = load_w_fm(E, W["L1_attn_w_O"], 512, 1024, "wo1")
        vext = E.sbuf("vextsb", [128, 33, EN, 64], BF16)
        E.dma(vext[:], vext_dram.ap().rearrange("(g p) e d -> p g e d", p=128))
        for g in range(32):
            with E.scope():
                Yfm = E.sbuf("yfm1", [128, 4, 128], BF16)
                for h in range(HN):
                    with E.scope():
                        psc = [E.psum("scps", [128, 3, 160], F32),
                               E.psum("scps2", [128, 1, 160], F32)]
                        hb = (h % 2) * 64
                        for e in range(EN):
                            dst = psc[0][:, e, :] if e < 3 else psc[1][:, 0, :]
                            E.mm(dst, hslice_l1(Q_fm, h,
                                                slice(g * 128, (g + 1) * 128)),
                                 nKV_bf[hb:hb + 64, e, g * 128:g * 128 + 160])
                        sm = E.sbuf("sm1", [128, 4, 160], F32)
                        E.vop("tensor_add", sm[:, 0:3, :], psc[0][:],
                              bmadd[:, None, :].broadcast_to([128, 3, 160]))
                        E.vop("tensor_add", sm[:, 3:4, :], psc[1][:],
                              bmadd[:, None, :].broadcast_to([128, 1, 160]))
                        mx = E.sbuf("mx1", [128, 4], F32)
                        E.vop("tensor_reduce", mx[:], sm[:],
                              axis=mybir.AxisListType.X, op=AL.max)
                        mx1 = E.sbuf("mx1b", [128, 1], F32)
                        E.vop("tensor_reduce", mx1[:], mx[:],
                              axis=mybir.AxisListType.X, op=AL.max)
                        E.vop("tensor_scalar_mul", mx1[:], mx1[:], -1.0)
                        ssum = E.sbuf("ssum1", [128, 1], F32)
                        E.act(sm[:].rearrange("p e j -> p (e j)"),
                              sm[:].rearrange("p e j -> p (e j)"), AF.Exp,
                              bias=mx1[:], accum_out=ssum[:])
                        rec = E.sbuf("rec1", [128, 1], F32)
                        E.vop("reciprocal", rec[:], ssum[:])
                        pb = E.sbuf("pb1", [128, 4, 160], BF16)
                        E.vop("tensor_scalar", pb[:], sm[:], rec[:], None,
                              op0=AL.mult)
                        pTa = E.sbuf("pta1", [128, 4, 128], BF16)
                        pTb = E.sbuf("ptb1", [128, 4, 128], BF16)
                        E.memset(pTb[32:64, :, :], 0.0)
                        E.memset(pTb[64:128, :, :], 0.0)
                        psta = E.psum("ptps", [128, 512], BF16)
                        pstb = E.psum("ptps2", [32, 4, 128], BF16)
                        for e in range(EN):
                            E.trans(psta[:, e * 128:(e + 1) * 128],
                                    pb[:, e, 0:128])
                            E.trans(pstb[0:32, e, :], pb[:, e, 128:160])
                        E.vop("tensor_copy", pTa[:],
                              psta[:].rearrange("p (e j) -> p e j", j=128))
                        E.vop("tensor_copy", pTb[0:32, :, :], pstb[0:32, :, :])
                        psy = E.psum("yps1", [64, 128], F32)
                        for e in range(EN):
                            E.mm(psy[0:64, :], vext[:, g, e, :], pTa[:, e, :],
                                 start=(e == 0), stop=False)
                            E.mm(psy[0:64, :], vext[:, g + 1, e, :],
                                 pTb[:, e, :], start=False,
                                 stop=(e == EN - 1))
                        E.act(Yfm[(h % 2) * 64:(h % 2) * 64 + 64, h // 2, :],
                              psy[0:64, :], AF.Identity)
                ps = [E.psum("prj1", [128, 512], F32) for _ in range(2)]
                for nh in range(2):
                    for c in range(4):
                        E.mm(ps[nh][:], Yfm[:, c, :],
                             wO[:, c, nh * 512:(nh + 1) * 512],
                             start=(c == 0), stop=(c == 3))
                xres = E.sbuf("x1ares", [128, 1024], F32)
                E.dma(xres[:], xin.ap()[g * 128:(g + 1) * 128, :])
                xnew = E.sbuf("xnew1a", [128, 1024], F32)
                for nh in range(2):
                    sl = slice(nh * 512, (nh + 1) * 512)
                    E.vop("tensor_add", xnew[:, sl], ps[nh][:], xres[:, sl])
                og = E.sbuf("og1a", [128, 1024], F32)
                emit_ln(E, og[:], xnew[:], 1024, 1e-5, prefix="oln1a")
                E.dma(xout.ap()[g * 128:(g + 1) * 128, :], og[:])
    E.mark_output(xout.ap())



def _l2f_tables(E):
    DH, WIN = 64, 256
    half = np.arange(0, DH, 2, dtype=np.float64)
    invf = (1.0 / 10000.0 ** (half / DH)).astype(np.float32)
    base_scale = ((half + 0.4 * DH) / (1.4 * DH)).astype(np.float32)
    invf_bc = E.const_sb(np.tile(invf, (128, 1)), "l2f_invf")
    lnbs_bc = E.const_sb(np.tile(np.log(base_scale).astype(np.float32),
                                 (128, 1)), "l2f_lnbs")
    tabs = {nm: E.sbuf("tab_" + nm, [128, 32, 64], BF16) for nm in
            ("cs", "sn", "ci", "si")}
    INV2PI = float(1.0 / (2 * np.pi))
    for g in range(32):
        with E.scope():
            tcol = E.sbuf("tcol", [128, 1], I32)
            E.op("gpsimd", lambda e, o=tcol[:], g=g: e.iota(
                o, pattern=[[0, 1]], base=g * 128, channel_multiplier=1),
                reads=(), writes=(tcol[:],))
            tcf = E.sbuf("tcf", [128, 1], F32)
            E.vop("tensor_copy", tcf[:], tcol[:])
            fr = E.sbuf("fr", [128, 32], F32)
            E.vop("tensor_scalar", fr[:], invf_bc[:], tcf[:], None,
                  op0=AL.mult)
            pw = E.sbuf("pw", [128, 1], F32)
            E.vop("tensor_scalar", pw[:], tcf[:], -2048.0, 1.0 / (WIN // 2),
                  op0=AL.add, op1=AL.mult)
            sc = E.sbuf("sc", [128, 32], F32)
            E.vop("tensor_scalar", sc[:], lnbs_bc[:], pw[:], None,
                  op0=AL.mult)
            E.act(sc[:], sc[:], AF.Exp)
            isc = E.sbuf("isc", [128, 32], F32)
            E.vop("reciprocal", isc[:], sc[:])

            def sincos(dst_cos, dst_sin, scale_tile):
                y = E.sbuf("y", [128, 32], F32)
                E.vop("tensor_scalar_mul", y[:], fr[:], INV2PI)
                yc = E.sbuf("yc", [128, 32], F32)
                E.vop("tensor_scalar_add", yc[:], y[:], 0.25)
                for src, dst in ((y, dst_sin), (yc, dst_cos)):
                    ii = E.sbuf("ii", [128, 32], I32)
                    E.vop("tensor_copy", ii[:], src[:])
                    ff = E.sbuf("ff", [128, 32], F32)
                    E.vop("tensor_copy", ff[:], ii[:])
                    E.vop("tensor_sub", ff[:], src[:], ff[:])
                    s1 = E.sbuf("s1v", [128, 32], F32)
                    E.act(s1[:], ff[:], AF.Sin, scale=float(np.pi))
                    av = E.sbuf("av", [128, 32], F32)
                    E.act(av[:], ff[:], AF.Abs)
                    c1 = E.sbuf("c1v", [128, 32], F32)
                    E.act(c1[:], av[:], AF.Sin, scale=float(-np.pi),
                          bias=float(np.pi / 2))
                    sn = E.sbuf("snv", [128, 32], F32)
                    E.vop("tensor_mul", sn[:], s1[:], c1[:])
                    E.vop("tensor_scalar_mul", sn[:], sn[:], 2.0)
                    E.vop("tensor_mul", sn[:], sn[:], scale_tile[:])
                    for hh in range(2):
                        E.vop("tensor_copy",
                              dst[:, g, hh * 32:(hh + 1) * 32], sn[:])

            sincos(tabs["cs"][:], tabs["sn"][:], sc)
            sincos(tabs["ci"][:], tabs["si"][:], isc)
    return tabs


def _l2f_attn(E, xin, xnext, Wqkv_d, Wo_d, qs_d, ks_d, li, tabs):
    WIN = 256
    Wqkv = E.sbuf("wqkv2", [128, 8, 3072], BF16)
    load_w_fm(E, Wqkv_d, 1024, 3072, row0=li * 1024, row_stride=3072,
              into=Wqkv)
    Wo = E.sbuf("wo2f", [128, 8, 1024], BF16)
    load_w_fm(E, Wo_d, 1024, 1024, row0=li * 1024, row_stride=1024, into=Wo)
    qsr = E.sbuf("qsr", [1, 64], F32)
    E.dma(qsr[:], qs_d.ap()[li, None, :])
    ksr = E.sbuf("ksr", [1, 64], F32)
    E.dma(ksr[:], ks_d.ap()[li, None, :])
    qsb = bcast_row(E, qsr[:], 64, "qsb")
    ksb = bcast_row(E, ksr[:], 64, "ksb")
    E.vop("tensor_scalar_mul", qsb[:], qsb[:], 8.0)

    x_fm = E.sbuf("x2ffm", [128, 8, T + 1], BF16)
    E.memset(x_fm[:, :, 0:1], 0.0)
    for g in range(32):
        with E.scope():
            xt = E.sbuf("xt2f", [128, 1024], F32)
            E.dma(xt[:], xin.ap()[g * 128:(g + 1) * 128, :])
            transpose_to_fm(E, x_fm[:, :, 1:T + 1], [(xt[:], g * 128)], "l2ft")

    def sx(c, tsl):
        off = 1 if c < 4 else 0
        return x_fm[:, c, tsl.start + off:tsl.stop + off]

    masks = {}
    for sub in range(2):
        for w0 in range(2):
            m = np.full((128, 512), -1e9, np.float32)
            for i in range(128):
                qa = sub * 128 + i
                lo, hi = qa, qa + 256
                m[i, lo:hi + 1] = 0.0
            if w0:
                m[:, 0:256] = -1e9
            masks[(sub, w0)] = E.const_sb(m.astype(BF),
                                          f"l2f_mask_{sub}_{w0}")

    kbuf = [E.sbuf("kbuf", [128, 8, 256], BF16) for _ in range(2)]
    vbuf = [E.sbuf("vbuf", [128, 2, 16, 64], BF16) for _ in range(2)]
    E.memset(kbuf[1][:], 0.0)
    E.memset(vbuf[1][:], 0.0)
    for w in range(16):
        cur, prv = kbuf[w % 2], kbuf[(w + 1) % 2]
        vcur, vprv = vbuf[w % 2], vbuf[(w + 1) % 2]
        with E.scope():
            qf = E.sbuf("qf2", [128, 8, 256], BF16)
            rcp = E.sbuf("rcp2", [128, 2, 16], F32)
            for sub in range(2):
                g = w * 2 + sub
                tsl = slice(g * 128, (g + 1) * 128)
                with E.scope():
                    qkv = E.sbuf("qkv2", [128, 3072], BF16)
                    psp = [E.psum("qkps2", [128, 512], F32) for _ in range(2)]
                    for oc in range(6):
                        ps = psp[oc % 2]
                        for c in range(8):
                            E.mm(ps[:], sx(c, tsl),
                                 Wqkv[:, c, oc * 512:(oc + 1) * 512],
                                 start=(c == 0), stop=(c == 7))
                        E.act(qkv[:, oc * 512:(oc + 1) * 512], ps[:],
                              AF.Identity)
                    qn = E.sbuf("qn2", [128, 1024], BF16)
                    kn = E.sbuf("kn2", [128, 1024], BF16)
                    for which, scb, ct, st, dst in (
                            (0, qsb, "cs", "sn", qn), (1, ksb, "ci", "si", kn)):
                        src = qkv[:, which * 1024:(which + 1) * 1024]
                        sv = src.rearrange("p (h d) -> p h d", d=64)
                        sq = E.sbuf("sq2", [128, 1024], BF16)
                        rs = E.sbuf("rs2", [128, 16], F32)
                        E.act(sq[:], src, AF.Square)
                        E.vop("tensor_reduce", rs[:],
                              sq[:].rearrange("p (h d) -> p h d", d=64),
                              axis=mybir.AxisListType.X, op=AL.add)
                        E.vop("tensor_scalar_max", rs[:], rs[:], 1e-24)
                        E.act(rs[:], rs[:], AF.Ln)
                        E.act(rs[:], rs[:], AF.Exp, scale=-0.5)
                        t0 = E.sbuf("t02", [128, 16, 64], BF16)
                        E.vop("tensor_mul", t0[:], sv,
                              rs[:, :, None].broadcast_to([128, 16, 64]))
                        E.vop("tensor_mul", t0[:], t0[:],
                              scb[:, None, 0:64].broadcast_to([128, 16, 64]))
                        rh = E.sbuf("rh2", [128, 16, 64], BF16)
                        E.vop("tensor_scalar_mul", rh[:, :, 0:32],
                              t0[:, :, 32:64], -1.0)
                        E.vop("tensor_copy", rh[:, :, 32:64], t0[:, :, 0:32])
                        E.vop("tensor_mul", t0[:], t0[:],
                              tabs[ct][:, g, None, :].broadcast_to(
                                  [128, 16, 64]))
                        E.vop("scalar_tensor_tensor", rh[:], rh[:], 1.0,
                              tabs[st][:, g, None, :].broadcast_to(
                                  [128, 16, 64]), op0=AL.mult, op1=AL.mult)
                        E.vop("tensor_add",
                              dst[:].rearrange("p (h d) -> p h d", d=64),
                              t0[:], rh[:])
                    E.vop("tensor_copy", vcur[:, sub, :, :],
                          qkv[:, 2048:3072].rearrange("p (h d) -> p h d",
                                                      d=64))
                    transpose_to_fm(E, qf, [(qn[:], sub * 128)], "l2fq")
                    transpose_to_fm(E, cur, [(kn[:], sub * 128)], "l2fk")
            ao_tm = E.sbuf("ao2", [128, 2, 16, 64], BF16)
            for sub in range(2):
                with E.scope():
                    psS = [E.psum("sps2", [128, 512], F32) for _ in range(2)]
                    psP = [E.psum("pps2", [128, 512], BF16) for _ in range(2)]
                    psO = [E.psum("ops2", [128, 8, 64], F32) for _ in range(2)]
                    smb = [E.sbuf("smf", [128, 512], F32) for _ in range(2)]
                    mxb = [E.sbuf("mxf", [128, 1], F32) for _ in range(2)]
                    ssb = [E.sbuf("ssf", [128, 1], F32) for _ in range(2)]
                    pbb = [E.sbuf("pbf", [128, 512], BF16) for _ in range(2)]
                    pTb_ = [E.sbuf("pTf", [128, 4, 128], BF16)
                            for _ in range(2)]
                    for h in range(16):
                        hb = (h % 2) * 64
                        hc = h // 2
                        ps = psS[h % 2]
                        E.mm(ps[:, 0:256],
                             qf[hb:hb + 64, hc, sub * 128:(sub + 1) * 128],
                             prv[hb:hb + 64, hc, :])
                        E.mm(ps[:, 256:512],
                             qf[hb:hb + 64, hc, sub * 128:(sub + 1) * 128],
                             cur[hb:hb + 64, hc, :])
                        sm = smb[h % 2]
                        E.vop("tensor_add", sm[:], ps[:],
                              masks[(sub, 1 if w == 0 else 0)][:])
                        mx = mxb[h % 2]
                        E.vop("tensor_reduce", mx[:], sm[:],
                              axis=mybir.AxisListType.X, op=AL.max)
                        E.vop("tensor_scalar_mul", mx[:], mx[:], -1.0)
                        ssum = ssb[h % 2]
                        pb = pbb[h % 2]
                        E.act(pb[:], sm[:], AF.Exp, bias=mx[:],
                              accum_out=ssum[:])
                        E.vop("reciprocal", rcp[:, sub, h:h + 1], ssum[:])
                        pp = psP[h % 2]
                        for t4 in range(4):
                            E.trans(pp[:, t4 * 128:(t4 + 1) * 128],
                                    pb[:, t4 * 128:(t4 + 1) * 128])
                        pT = pTb_[h % 2]
                        E.vop("tensor_copy", pT[:],
                              pp[:].rearrange("p (i j) -> p i j", j=128))
                        po = psO[h // 8]
                        for t4 in range(4):
                            vb = vprv if t4 < 2 else vcur
                            E.mm(po[:, h % 8, :], pT[:, t4, :],
                                 vb[:, t4 % 2, h, :],
                                 start=(t4 == 0), stop=(t4 == 3))
                    for b2 in range(2):
                        E.vop("tensor_mul", ao_tm[:, sub, b2 * 8:b2 * 8 + 8, :],
                              psO[b2][:],
                              rcp[:, sub, b2 * 8:b2 * 8 + 8, None]
                              .broadcast_to([128, 8, 64]))
            for sub in range(2):
                g = w * 2 + sub
                with E.scope():
                    af = E.sbuf("af2", [128, 8, 128], BF16)
                    transpose_to_fm(E, af, [(ao_tm[:, sub, :, :].rearrange(
                        "p h d -> p (h d)"), 0)], "l2fa")
                    ps = [E.psum("prj2", [128, 512], F32) for _ in range(2)]
                    for nh in range(2):
                        for c in range(8):
                            E.mm(ps[nh][:], af[:, c, :],
                                 Wo[:, c, nh * 512:(nh + 1) * 512],
                                 start=(c == 0), stop=(c == 7))
                    xres = E.sbuf("xr2f", [128, 1024], F32)
                    E.dma(xres[:], xin.ap()[g * 128:(g + 1) * 128, :])
                    xnew = E.sbuf("xn2f", [128, 1024], F32)
                    for nh in range(2):
                        sl = slice(nh * 512, (nh + 1) * 512)
                        E.vop("tensor_add", xnew[:, sl], ps[nh][:],
                              xres[:, sl])
                    E.dma(xnext.ap()[g * 128:(g + 1) * 128, :], xnew[:])


def _l2f_ffn(E, xin, xnext, W, li):
    DI = 2730
    NCH = 22
    lng_r = E.sbuf("lng", [1, 1024], F32)
    E.dma(lng_r[:], W["L2_ffn_ln_g"].ap()[li, None, :])
    lnb_r = E.sbuf("lnb", [1, 1024], F32)
    E.dma(lnb_r[:], W["L2_ffn_ln_b"].ap()[li, None, :])
    lng = bcast_row(E, lng_r[:], 1024, "lngb")
    lnb = bcast_row(E, lnb_r[:], 1024, "lnbb")
    b2r = E.sbuf("b2r2", [1, 1024], F32)
    E.dma(b2r[:], W["L2_ffn_b2"].ap()[li, None, :])
    b2bc = bcast_row(E, b2r[:], 1024, "b2bc2")
    b1f = []
    for half in range(2):
        out = E.sbuf(f"b1f{half}", [128, NCH], F32)
        with E.scope():
            rows = E.sbuf("b1r", [NCH, 128], F32)
            E.memset(rows[:], 0.0)
            base = li * 5460 + half * DI
            E.dma(rows[0:21, :], bass.AP(tensor=W["L2_ffn_b1"], offset=base,
                                         ap=[[128, 21], [1, 128]]))
            E.dma(rows[21:22, 0:42],
                  bass.AP(tensor=W["L2_ffn_b1"], offset=base + 21 * 128,
                          ap=[[42, 1], [1, 42]]))
            rb = E.sbuf("b1rb", [NCH, 128], BF16)
            E.vop("tensor_copy", rb[:], rows[:])
            ps = E.psum("b1ps", [128, NCH], BF16)
            E.trans(ps[:], rb[0:NCH, :])
            E.act(out[:], ps[:], AF.Identity)
        b1f.append(out)
    W2sb = E.sbuf("w2sb2", [128, NCH, 1024], BF16)
    E.memset(W2sb[32:64, 21, :], 0.0)
    E.memset(W2sb[64:128, 21, :], 0.0)
    for ch in range(NCH):
        rows = 128 if ch < 21 else 42
        ap = bass.AP(tensor=W["L2_ffn_W2"],
                     offset=(li * DI + ch * 128) * 1024,
                     ap=[[1024, rows], [1, 1024]])
        E.dma(W2sb[0:rows, ch, :], ap, engine="gpsimd")

    w1db = [[E.sbuf("w1db2", [128, 8, 128], BF16) for _ in range(2)]
            for _ in range(2)]
    for pg in range(8):
        tsl = slice(pg * 512, (pg + 1) * 512)
        with E.scope():
            sxn_fm = E.sbuf("sxnfm", [128, 8, 512], BF16)
            for tt in range(4):
                with E.scope():
                    g = pg * 4 + tt
                    stm = E.sbuf("sxtm", [128, 1024], F32)
                    E.dma(stm[:, 0:512],
                          xin.ap()[g * 128:(g + 1) * 128, 0:512])
                    if g == 0:
                        E.memset(stm[0:1, 512:1024], 0.0)
                        E.dma(stm[1:128, 512:1024],
                              xin.ap()[0:127, 512:1024])
                    else:
                        E.dma(stm[:, 512:1024],
                              xin.ap()[g * 128 - 1:(g + 1) * 128 - 1,
                                       512:1024])
                    xn = E.sbuf("xnf2", [128, 1024], F32)
                    emit_ln(E, xn[:], stm[:], 1024, 1e-5,
                            scale_mul_bc=lng[:], bias_add_bc=lnb[:],
                            prefix="l2fln")
                    transpose_to_fm(E, sxn_fm, [(xn[:], tt * 128)], "l2fn")
            ag = E.sbuf("ag2", [128, NCH, 512], BF16)
            E.memset(ag[32:64, 21, :], 0.0)
            E.memset(ag[64:128, 21, :], 0.0)
            with E.scope():
                psa = E.psum("aps2", [128, 512], F32)
                psg = E.psum("gps2", [128, 512], F32)
                gsb = E.sbuf("gsb2", [128, 512], F32)
                for ch in range(NCH):
                    rows = 128 if ch < 21 else 42
                    wa = w1db[0][ch % 2]
                    wg = w1db[1][ch % 2]
                    apa = bass.AP(tensor=W["L2_ffn_W1"],
                                  offset=li * 1024 * 5460 + ch * 128,
                                  ap=[[5460, 128], [128 * 5460, 8],
                                      [1, rows]])
                    E.dma(wa[:, :, 0:rows], apa, engine="gpsimd")
                    apg = bass.AP(tensor=W["L2_ffn_W1"],
                                  offset=li * 1024 * 5460 + DI + ch * 128,
                                  ap=[[5460, 128], [128 * 5460, 8],
                                      [1, rows]])
                    E.dma(wg[:, :, 0:rows], apg, engine="gpsimd")
                    for c in range(8):
                        E.mm(psa[0:rows, :], wa[:, c, 0:rows], sxn_fm[:, c, :],
                             start=(c == 0), stop=(c == 7))
                    for c in range(8):
                        E.mm(psg[0:rows, :], wg[:, c, 0:rows], sxn_fm[:, c, :],
                             start=(c == 0), stop=(c == 7))
                    E.act(gsb[0:rows, :], psg[0:rows, :], AF.Gelu,
                          bias=b1f[1][0:rows, ch:ch + 1])
                    E.vop("tensor_scalar", ag[0:rows, ch, :], psa[0:rows, :],
                          b1f[0][0:rows, ch:ch + 1], None, op0=AL.add)
                    E.vop("tensor_mul", ag[0:rows, ch, :], ag[0:rows, ch, :],
                          gsb[0:rows, :])
            for tt in range(4):
                g = pg * 4 + tt
                with E.scope():
                    ps = [E.psum("yps2", [128, 512], F32) for _ in range(2)]
                    for nh in range(2):
                        for ch in range(NCH):
                            E.mm(ps[nh][:],
                                 ag[:, ch, tt * 128:(tt + 1) * 128],
                                 W2sb[:, ch, nh * 512:(nh + 1) * 512],
                                 start=(ch == 0), stop=(ch == NCH - 1))
                    xres = E.sbuf("xr2g", [128, 1024], F32)
                    E.dma(xres[:], xin.ap()[g * 128:(g + 1) * 128, :])
                    xnew = E.sbuf("xn2gg", [128, 1024], F32)
                    for nh in range(2):
                        sl = slice(nh * 512, (nh + 1) * 512)
                        E.vop("tensor_add", xnew[:, sl], ps[nh][:],
                              b2bc[:, sl])
                    E.vop("tensor_add", xnew[:], xnew[:], xres[:])
                    E.dma(xnext.ap()[g * 128:(g + 1) * 128, :], xnew[:])


def emit_l2_ffn(E, xin, xout, W):
    tabs = _l2f_tables(E)
    bufs = [E.dram(f"l2x{i}", [T, D], F32) for i in range(2)]
    cur = xin
    for li in range(6):
        nxt = bufs[0] if cur is not bufs[0] else bufs[1]
        with E.scope():
            _l2f_attn(E, cur, nxt, W["L2_ffn_Wqkv"], W["L2_ffn_Wo"],
                      W["L2_ffn_q_scale"], W["L2_ffn_k_scale"], li, tabs)
        cur, nxt = nxt, (bufs[0] if nxt is bufs[1] else bufs[1])
        with E.scope():
            _l2f_ffn(E, cur, nxt, W, li)
        cur = nxt
    for g in range(32):
        with E.scope():
            a = E.sbuf("fina", [128, 1024], F32)
            E.dma(a[:], xin.ap()[g * 128:(g + 1) * 128, :])
            b_ = E.sbuf("finb", [128, 1024], F32)
            E.dma(b_[:], cur.ap()[g * 128:(g + 1) * 128, :])
            E.vop("tensor_add", a[:], a[:], b_[:])
            og = E.sbuf("ogf2", [128, 1024], F32)
            emit_ln(E, og[:], a[:], 1024, 1e-5, prefix="olnf2")
            E.dma(xout.ap()[g * 128:(g + 1) * 128, :], og[:])
    E.mark_output(xout.ap())



def emit_l0_ffn(E, xin, xout, W):
    EN, CAP, HID = 8, 1024, 4096
    ones_f = E.const_sb(np.ones((1, 128), np.float32), "ones_row1")
    ones_full = E.const_sb(np.ones((128, 128), np.float32), "ones_full")
    ltriX = np.zeros((128, 128), np.float32)
    for p in range(128):
        ltriX[p, p + 1:] = 1.0
    ltriX_c = E.const_sb(ltriX, "ltri_excl")
    tsel = np.zeros((256, 256), np.float32)
    for g in range(32):
        for e in range(EN):
            for g2 in range(g + 1, 32):
                tsel[g * 8 + e, g2 * 8 + e] = 1.0
    tsel_c = [E.const_sb(tsel[i * 128:(i + 1) * 128, j * 128:(j + 1) * 128],
                         f"tsel_{i}{j}") for i in range(2) for j in range(2)]

    idx_dram = E.dram("moe_idx", [EN, CAP + 1], I32)
    gate_dram = E.dram("moe_gate", [T, EN], F32)
    ctot_dram = E.dram("moe_ctot", [256], F32)
    xacc = E.dram("moe_xacc", [T, D], F32)
    E.dma(xacc.ap()[:, :], xin.ap()[:, :])

    probs = E.sbuf("probs", [128, 32, EN], F32, persistent=False)
    posm = E.sbuf("posm", [128, 32, EN], I32)
    with E.scope():
        idf = E.const_sb(np.eye(128, dtype=np.float32), "ident_f32")
        gwf = E.sbuf("gwf", [128, 8, EN], F32)
        E.dma(gwf[:], W["L0_ffn_gate_w"].ap().rearrange(
            "(c p) e -> p c e", p=128))
        with E.scope():
            ps = E.psum("scps0", [128, EN], F32)
            xtf = E.sbuf("xtf0", [128, 8, 128], F32)
            pst = [E.psum("tfps0", [128, 128], F32) for _ in range(2)]
            for g in range(32):
                xt = E.sbuf("xt0f", [128, 1024], F32)
                E.dma(xt[:], xin.ap()[g * 128:(g + 1) * 128, :])
                for c in range(8):
                    pp = pst[c % 2]

                    def emit_t(eng, o=pp[:], i_=xt[:, c * 128:(c + 1) * 128]):
                        return eng.transpose(o, i_, idf[:])
                    E.op("tensor", emit_t, reads=(xt[:, c * 128:(c + 1) * 128],
                                                  idf[:]), writes=(pp[:],))
                    E.act(xtf[:, c, :], pp[:], AF.Identity)
                for c in range(8):
                    E.mm(ps[:], xtf[:, c, :],
                         gwf[:, c, :], start=(c == 0), stop=(c == 7))
                sc = E.sbuf("sc0", [128, EN], F32)
                mx = E.sbuf("mx0", [128, 1], F32)
                E.vop("tensor_reduce", mx[:], ps[:],
                      axis=mybir.AxisListType.X, op=AL.max)
                E.vop("tensor_scalar_mul", mx[:], mx[:], -1.0)
                sm = E.sbuf("sm0", [128, 1], F32)
                E.act(sc[:], ps[:], AF.Exp, bias=mx[:], accum_out=sm[:])
                E.vop("reciprocal", sm[:], sm[:])
                E.vop("tensor_scalar", probs[:, g, :], sc[:], sm[:], None,
                      op0=AL.mult)
                E.dma(gate_dram.ap()[g * 128:(g + 1) * 128, :],
                      probs[:, g, :])
        lo = E.sbuf("lo0", [128, EN], F32)
        hi = E.sbuf("hi0", [128, EN], F32)
        E.memset(lo[:], 0.0)
        E.memset(hi[:], 1.0)
        pv = probs[:].rearrange("p g e -> p e g")
        onesb128 = E.sbuf("onesb128", [128, 128], BF16, persistent=True)
        E.vop("tensor_copy", onesb128[:], ones_full[:])
        for it in range(42):
            with E.scope():
                mid = E.sbuf("mid0", [128, EN], F32)
                E.vop("tensor_add", mid[:], lo[:], hi[:])
                E.vop("tensor_scalar_mul", mid[:], mid[:], 0.5)
                ge = E.sbuf("ge0", [128, EN, 32], F32)
                E.vop("tensor_sub", ge[:], pv,
                      mid[:, :, None].broadcast_to([128, EN, 32]))
                E.vop("tensor_single_scalar", ge[:], ge[:], 0.0, op=AL.is_ge)
                cnt = E.sbuf("cnt0", [128, EN], F32)
                E.vop("tensor_reduce", cnt[:], ge[:],
                      axis=mybir.AxisListType.X, op=AL.add)
                pst = E.psum("ctps", [128, EN], F32)

                def emit_t(eng, o=pst[:], i_=ones_full[:], r=cnt[:]):
                    return eng.matmul(o, i_, r, start=True, stop=True)
                E.op("tensor", emit_t, reads=(ones_full[:], cnt[:]),
                     writes=(pst[:],))
                cond = E.sbuf("cond0", [128, EN], F32)
                E.vop("tensor_single_scalar", cond[:], pst[:], float(CAP),
                      op=AL.is_ge)
                t = E.sbuf("tt0", [128, EN], F32)
                E.vop("tensor_sub", t[:], mid[:], lo[:])
                E.vop("tensor_mul", t[:], t[:], cond[:])
                E.vop("tensor_add", lo[:], lo[:], t[:])
                E.vop("tensor_sub", t[:], hi[:], mid[:])
                E.vop("tensor_mul", t[:], t[:], cond[:])
                E.vop("tensor_add", hi[:], mid[:], t[:])
        mask = E.sbuf("mask0", [128, 32, EN], F32)
        E.vop("tensor_sub", mask[:], probs[:],
              lo[:, None, :].broadcast_to([128, 32, EN]))
        E.vop("tensor_single_scalar", mask[:], mask[:], 0.0, op=AL.is_ge)
        maskb = E.sbuf("maskb", [128, 256], BF16)
        E.vop("tensor_copy", maskb[:], mask[:].rearrange("p g e -> p (g e)"))
        wp = E.sbuf("wp0", [128, 256], F32)
        with E.scope():
            psw = E.psum("wpps", [128, 256], F32)
            mf = mask[:].rearrange("p g e -> p (g e)")

            def emit_w(eng, o=psw[:], l=ltriX_c[:], r=mf):
                return eng.matmul(o, l, r, start=True, stop=True)
            E.op("tensor", emit_w, reads=(ltriX_c[:], mask[:]),
                 writes=(psw[:],))
            E.act(wp[:], psw[:], AF.Identity)
            totc = E.sbuf("totc", [128, 2], F32)
            pst2 = E.psum("totps", [128, 2], F32)
            for ch in range(2):
                def emit_tc(eng, o=pst2[:, ch:ch + 1],
                            l=mf[:, ch * 128:(ch + 1) * 128],
                            r=ones_full[:, 0:1]):
                    return eng.matmul(o, l, r, start=True, stop=True)
                E.op("tensor", emit_tc, reads=(mask[:], ones_full[:]),
                     writes=(pst2[:],))
            E.act(totc[:], pst2[:], AF.Identity)
            psc = E.psum("ctps2", [128, 2], F32)
            for j in range(2):
                for i in range(2):
                    def emit_ts(eng, o=psc[:, j:j + 1],
                                l=tsel_c[i * 2 + j][:],
                                r=totc[:, i:i + 1], s=(i == 0), st=(i == 1)):
                        return eng.matmul(o, l, r, start=s, stop=st)
                    E.op("tensor", emit_ts,
                         reads=(tsel_c[i * 2 + j][:], totc[:]),
                         writes=(psc[:],))
            ctotc = E.sbuf("ctotc", [128, 2], F32)
            E.act(ctotc[:], psc[:], AF.Identity)
            pstr = E.psum("ctrps", [2, 128], F32)
            idf = E.const_sb(np.eye(128, dtype=np.float32), "ident_f32")

            def emit_tr(eng, o=pstr[0:2, :], i_=ctotc[:]):
                return eng.transpose(o, i_, idf[:])
            E.op("tensor", emit_tr, reads=(ctotc[:], idf[:]),
                 writes=(pstr[:],))
            ctr = E.sbuf("ctr", [2, 128], F32)
            E.act(ctr[0:2, :], pstr[0:2, :], AF.Identity)
            E.dma(ctot_dram.ap()[0:128][None, :], ctr[0:1, :])
            E.dma(ctot_dram.ap()[128:256][None, :], ctr[1:2, :])
        ctrow = E.sbuf("ctrow", [1, 256], F32)
        E.dma(ctrow[:], ctot_dram.ap()[None, :])
        ctbc = bcast_row(E, ctrow[:], 256, "ctbc")
        posf = E.sbuf("posf", [128, 256], F32)
        E.vop("tensor_add", posf[:], wp[:], ctbc[:])
        t2 = E.sbuf("t20", [128, 256], F32)
        E.vop("tensor_scalar_sub", t2[:], posf[:], float(CAP))
        E.vop("tensor_mul", t2[:], t2[:],
              mask[:].rearrange("p g e -> p (g e)"))
        E.vop("tensor_scalar_add", t2[:], t2[:], float(CAP))
        E.vop("tensor_copy", posm[:].rearrange("p g e -> p (g e)"), t2[:])
        iota_g = E.sbuf("iotag", [128, 32], I32)
        E.op("gpsimd", lambda e, o=iota_g[:]: e.iota(
            o, pattern=[[128, 32]], base=0, channel_multiplier=1),
            reads=(), writes=(iota_g[:],))
        idx_flat = idx_dram.ap().rearrange("e c -> (e c)")[:, None]
        for e in range(EN):
            for g in range(32):
                E.idma(idx_flat, iota_g[:, g:g + 1],
                       out_off=IndirectOffsetOnAxis(posm[:, g, e:e + 1], 0),
                       element_offset=e * (CAP + 1))

    w2sb = E.sbuf("w2sb0", [128, 32, 1024], BF16)
    w1db = [E.sbuf("w1db0", [128, 8, 128], BF16) for _ in range(2)]
    for e in range(EN):
        with E.scope():
            idxc = E.sbuf("idxc", [128, 8], I32)
            E.dma(idxc[:], idx_dram.ap()[e, 0:CAP].rearrange(
                "(p q) -> p q", q=8))
            load_w_fm(E, W["L0_ffn_w2"], 4096, 1024, row0=e * 4096,
                      row_stride=1024, into=w2sb)
            xg = E.sbuf("xg0", [128, 8, 1024], BF16)
            gcmp = E.sbuf("gcmp", [128, 8, EN], F32)
            for q in range(8):
                E.idma(xg[:, q, :], xin.ap()[:, :],
                       in_off=IndirectOffsetOnAxis(idxc[:, q:q + 1], 0))
                E.idma(gcmp[:, q, :], gate_dram.ap()[:, :],
                       in_off=IndirectOffsetOnAxis(idxc[:, q:q + 1], 0))
            xgf = E.sbuf("xgf0", [128, 8, 1024], BF16)
            for q in range(8):
                with E.scope():
                    for half in range(2):
                        pst = E.psum("xgps", [128, 512], BF16)
                        for i in range(4):
                            c = half * 4 + i
                            E.trans(pst[:, i * 128:(i + 1) * 128],
                                    xg[:, q, c * 128:(c + 1) * 128])
                        E.act(xgf[:, half * 4:half * 4 + 4,
                                  q * 128:(q + 1) * 128],
                              pst[:].rearrange("p (i j) -> p i j", j=128),
                              AF.Identity)
            for half in range(2):
                csl = slice(half * 512, (half + 1) * 512)
                with E.scope():
                    hsb = E.sbuf("hsb0", [128, 32, 512], BF16)
                    psp = [E.psum("hps0", [128, 512], F32) for _ in range(2)]
                    for hc in range(32):
                        wb = w1db[hc % 2]
                        apw = bass.AP(tensor=W["L0_ffn_w1"],
                                      offset=e * 1024 * HID + hc * 128,
                                      ap=[[HID, 128], [128 * HID, 8],
                                          [1, 128]])
                        E.dma(wb[:], apw, engine="gpsimd")
                        ps = psp[hc % 2]
                        for c in range(8):
                            E.mm(ps[:], wb[:, c, :], xgf[:, c, csl],
                                 start=(c == 0), stop=(c == 7))
                        E.act(hsb[:, hc, :], ps[:], AF.Gelu)
                    for q4 in range(4):
                        q = half * 4 + q4
                        with E.scope():
                            ps = [E.psum("yps0", [128, 512], F32)
                                  for _ in range(2)]
                            for nh in range(2):
                                for hc in range(32):
                                    E.mm(ps[nh][:],
                                         hsb[:, hc, q4 * 128:(q4 + 1) * 128],
                                         w2sb[:, hc,
                                              nh * 512:(nh + 1) * 512],
                                         start=(hc == 0), stop=(hc == 31))
                            yt = E.sbuf("yt0", [128, 1024], F32)
                            for nh in range(2):
                                E.vop("tensor_scalar",
                                      yt[:, nh * 512:(nh + 1) * 512],
                                      ps[nh][:], gcmp[:, q, e:e + 1], None,
                                      op0=AL.mult)
                            E.idma(xacc.ap()[:, :], yt[:],
                                   out_off=IndirectOffsetOnAxis(
                                       idxc[:, q:q + 1], 0),
                                   compute_op=AL.add)
    for g in range(32):
        with E.scope():
            a = E.sbuf("fin0", [128, 1024], F32)
            E.dma(a[:], xacc.ap()[g * 128:(g + 1) * 128, :])
            og = E.sbuf("ogf0", [128, 1024], F32)
            emit_ln(E, og[:], a[:], 1024, 1e-5, prefix="olnf0")
            E.dma(xout.ap()[g * 128:(g + 1) * 128, :], og[:])
    E.mark_output(xout.ap())



BLOCKS = [
    ("L0_attn", ["L0_attn_ln1_scale", "L0_attn_ln1_bias", "L0_attn_wq",
                 "L0_attn_wk", "L0_attn_wv", "L0_attn_wo",
                 "L0_attn_ln2_scale", "L0_attn_ln2_bias", "L0_attn_w1",
                 "L0_attn_b1", "L0_attn_w2", "L0_attn_b2"]),
    ("L0_ffn", ["L0_ffn_gate_w", "L0_ffn_w1", "L0_ffn_w2"]),
    ("L1_attn", ["L1_attn_w_G", "L1_attn_b_G", "L1_attn_w_K", "L1_attn_w_V",
                 "L1_attn_w_Q", "L1_attn_w_O", "L1_attn_init_K_rec",
                 "L1_attn_init_V_rec"]),
    ("L1_ffn", ["L1_ffn_W_r", "L1_ffn_W_k", "L1_ffn_W_v", "L1_ffn_mu_r",
                "L1_ffn_mu_k"]),
    ("L2_attn_partial", ["L2_attn_Wk", "L2_attn_Wv"]),
    ("L2_attn_rest", ["L2_attn_Wq"]),
    ("L2_ffn", ["L2_ffn_Wqkv", "L2_ffn_q_scale", "L2_ffn_k_scale",
                "L2_ffn_Wo", "L2_ffn_ln_g", "L2_ffn_ln_b", "L2_ffn_W1",
                "L2_ffn_b1", "L2_ffn_W2", "L2_ffn_b2"]),
    ("L3_attn", ["L3_attn_W_qkv", "L3_attn_W_out"]),
    ("L3_ffn", ["L3_ffn_U1", "L3_ffn_V1", "L3_ffn_b1", "L3_ffn_U2",
                "L3_ffn_V2", "L3_ffn_b2"]),
]

EMITTERS = {}


def _register_emitters():
    EMITTERS.update({
        "L0_attn": emit_l0_attn, "L0_ffn": emit_l0_ffn,
        "L1_attn": emit_l1_attn, "L1_ffn": emit_l1_ffn,
        "L2_ffn": emit_l2_ffn,
        "L3_attn": emit_l3_attn, "L3_ffn": emit_l3_ffn,
    })


def _build_block(name, wnames, wshapes):
    E = EB(name)
    nc = E.nc
    xin = nc.dram_tensor("xin", [T, D], F32, kind="ExternalInput")
    W = {k: nc.dram_tensor(k, list(wshapes[k]), F32, kind="ExternalInput")
         for k in wnames}
    if name == "L2_attn_partial":
        m_part = nc.dram_tensor("m_part_out", [1024, T], F32,
                                kind="ExternalOutput")
        emit_l2_attn_partial(E, xin, m_part, W)
    elif name == "L2_attn_rest":
        mparts = [nc.dram_tensor(f"mp{b}", [1024, T], F32,
                                 kind="ExternalInput") for b in range(B)]
        xout = nc.dram_tensor("xout", [T, D], F32, kind="ExternalOutput")
        emit_l2_attn_rest(E, xin, mparts, xout, W)
    else:
        xout = nc.dram_tensor("xout", [T, D], F32, kind="ExternalOutput")
        EMITTERS[name](E, xin, xout, W)
    return E.finalize()


def kernel(**inputs):
    _register_emitters()
    x = np.ascontiguousarray(np.asarray(inputs["L0_attn_inputs"],
                                        dtype=np.float32))
    weights = {k: np.ascontiguousarray(np.asarray(v, dtype=np.float32))
               for k, v in inputs.items() if k != "L0_attn_inputs"}
    cur = [np.ascontiguousarray(x[b]) for b in range(B)]
    mparts = None
    for name, wnames in BLOCKS:
        wshapes = {k: weights[k].shape for k in wnames}
        nc = _build_block(name, wnames, wshapes)
        in_maps = [dict({"xin": cur[b]}, **{k: weights[k] for k in wnames})
                   for b in range(B)]
        if name == "L2_attn_rest":
            for b in range(B):
                for j in range(B):
                    in_maps[b][f"mp{j}"] = mparts[j]
        res = run_bass_kernel_spmd(nc, in_maps, core_ids=list(range(N_CORES)))
        if name == "L2_attn_partial":
            mparts = [np.asarray(r["m_part_out"], dtype=np.float32)
                      for r in res.results]
        else:
            cur = [np.asarray(r["xout"], dtype=np.float32)
                   for r in res.results]
    return np.stack(cur).astype(np.float32)
